# version 11
# speedup vs baseline: 1.0223x; 1.0013x over previous
import math
import numpy as np
import ml_dtypes
from contextlib import ExitStack

import concourse.bass as bass
import concourse.mybir as mybir

F32 = mybir.dt.float32
BF16 = mybir.dt.bfloat16
AF = mybir.ActivationFunctionType
ALU = mybir.AluOpType
AX = mybir.AxisListType

ENGINES = ("tensor", "vector", "scalar", "gpsimd", "sync")

DEBUG_BUFS = {}
DEBUG_LI = 0


NDMASEM = 8


class Op:
    __slots__ = ("eng", "fn", "reads", "writes", "idx", "is_dma", "marked",
                 "inc_val", "cum", "waits", "tag", "_deps", "sem_name",
                 "throttle")

    def __init__(self, eng, fn, reads, writes, is_dma, tag):
        self.eng = eng
        self.fn = fn
        self.reads = reads
        self.writes = writes
        self.is_dma = is_dma
        self.marked = False
        self.waits = []
        self.tag = tag
        self._deps = []
        self.sem_name = None
        self.cum = None
        self.throttle = None


class Barrier:
    __slots__ = ("dma_snapshot",)

    def __init__(self, dma_snapshot):
        self.dma_snapshot = dma_snapshot


class Builder:

    def __init__(self, nc):
        self.nc = nc
        self.ops = []
        self.last_writer = {}
        self.readers = {}
        self.dma_slot = {e: 0 for e in ENGINES}
        self.dma_slot_cnt = {e: [0] * NDMASEM for e in ENGINES}

    def op(self, eng, fn, reads=(), writes=(), is_dma=False, tag=""):
        o = Op(eng, fn, tuple(reads), tuple(writes), is_dma, tag)
        o.idx = len(self.ops)
        deps = []
        for k in o.reads:
            w = self.last_writer.get(k)
            if w is not None:
                deps.append(w)
            if isinstance(k, tuple) and k and k[0] in ("psf", "psb"):
                for r in self.readers.get(k, ()):
                    if r is not o and r.eng != eng:
                        deps.append(r)
        for k in o.writes:
            w = self.last_writer.get(k)
            if w is not None:
                deps.append(w)
            for r in self.readers.get(k, ()):
                if r is not o:
                    deps.append(r)
        for d in deps:
            d.marked = True
        o._deps = deps
        if is_dma:
            slot = self.dma_slot[eng] % NDMASEM
            self.dma_slot[eng] += 1
            cnts = self.dma_slot_cnt[eng]
            o.sem_name = f"D_{eng}{slot}"
            if cnts[slot] > 0:
                o.throttle = (o.sem_name, 16 * cnts[slot])
            cnts[slot] += 1
            o.cum = 16 * cnts[slot]
            o.inc_val = 16
        self.ops.append(o)
        for k in o.reads:
            self.readers.setdefault(k, []).append(o)
        for k in o.writes:
            self.last_writer[k] = o
            self.readers[k] = [o] if k in o.reads else []
        return o

    def t(self, fn, reads=(), writes=(), tag=""):
        return self.op("tensor", fn, reads, writes, tag=tag)

    def v(self, fn, reads=(), writes=(), tag=""):
        return self.op("vector", fn, reads, writes, tag=tag)

    def s(self, fn, reads=(), writes=(), tag=""):
        return self.op("scalar", fn, reads, writes, tag=tag)

    def g(self, fn, reads=(), writes=(), tag=""):
        return self.op("gpsimd", fn, reads, writes, tag=tag)

    def dma(self, queue, out, in_, reads=(), writes=(), tag=""):
        def fn(eng, out=out, in_=in_):
            return eng.dma_start(out=out, in_=in_)
        return self.op(queue, fn, reads, writes, is_dma=True, tag=tag)

    def barrier(self):
        snap = {e: list(self.dma_slot_cnt[e]) for e in ENGINES}
        self.ops.append(Barrier(snap))
        self.last_writer = {}
        self.readers = {}

    def braid(self, bodies):
        start = len(self.ops)
        dma_slot_snap = dict(self.dma_slot)
        dma_cnt_snap = {e: list(c) for e, c in self.dma_slot_cnt.items()}
        streams = []
        for body in bodies:
            s0 = len(self.ops)
            body()
            streams.append(self.ops[s0:len(self.ops)])
        seg = self.ops[start:]
        assert not any(isinstance(o, Barrier) for o in seg), \
            "braid bodies must not emit barriers"
        del self.ops[start:]
        self.dma_slot = dma_slot_snap
        self.dma_slot_cnt = dma_cnt_snap
        seg_ids = set(id(o) for o in seg)
        done = set()
        ptrs = [0] * len(streams)
        nxt = 0
        remaining = len(seg)
        while remaining:
            emitted = False
            for k in range(len(streams)):
                lane = (nxt + k) % len(streams)
                i = ptrs[lane]
                if i >= len(streams[lane]):
                    continue
                o = streams[lane][i]
                if all(id(dep) not in seg_ids or id(dep) in done
                       for dep in o._deps):
                    ptrs[lane] += 1
                    done.add(id(o))
                    remaining -= 1
                    nxt = lane + 1
                    o.idx = len(self.ops)
                    if o.is_dma:
                        slot = self.dma_slot[o.eng] % NDMASEM
                        self.dma_slot[o.eng] += 1
                        cnts = self.dma_slot_cnt[o.eng]
                        o.sem_name = f"D_{o.eng}{slot}"
                        o.throttle = ((o.sem_name, 16 * cnts[slot])
                                      if cnts[slot] > 0 else None)
                        cnts[slot] += 1
                        o.cum = 16 * cnts[slot]
                    self.ops.append(o)
                    emitted = True
                    break
            assert emitted, "braid merge stuck (dependency cycle?)"

    def realize(self):
        nc = self.nc
        cum_cmp = {e: 0 for e in ENGINES}
        for o in self.ops:
            if isinstance(o, Barrier) or o.is_dma:
                continue
            if o.marked:
                cum_cmp[o.eng] += 1
                o.cum = cum_cmp[o.eng]
                o.sem_name = "C_" + o.eng
                o.inc_val = 1
            else:
                o.inc_val = 0
        for o in self.ops:
            if isinstance(o, Barrier):
                continue
            seen = {}
            for d in o._deps:
                assert d.cum is not None, f"dep of {o.tag!r} on unmarked {d.tag!r}"
                if seen.get(d.sem_name, -1) < d.cum:
                    seen[d.sem_name] = d.cum
            o.waits = sorted(seen.items())
        engine_seen = {e: {} for e in ENGINES}
        for o in self.ops:
            if isinstance(o, Barrier):
                merged = {}
                for e in ENGINES:
                    for s, v in engine_seen[e].items():
                        if merged.get(s, 0) < v:
                            merged[s] = v
                for qe, cnts in o.dma_snapshot.items():
                    for slot, c in enumerate(cnts):
                        if c:
                            sname = f"D_{qe}{slot}"
                            if merged.get(sname, 0) < 16 * c:
                                merged[sname] = 16 * c
                for e in ENGINES:
                    engine_seen[e] = dict(merged)
                continue
            seen = engine_seen[o.eng]
            ws = []
            for s, v in o.waits:
                if seen.get(s, 0) < v:
                    ws.append((s, v))
                    seen[s] = v
            o.waits = ws
            if o.throttle is not None:
                s, v = o.throttle
                if seen.get(s, 0) >= v:
                    o.throttle = None
                else:
                    seen[s] = v

        with ExitStack() as ctx:
            sems = {}
            for e in ENGINES:
                sems["C_" + e] = ctx.enter_context(nc.semaphore("C_" + e))
                for i in range(NDMASEM):
                    sems[f"D_{e}{i}"] = ctx.enter_context(
                        nc.semaphore(f"D_{e}{i}"))
            barsem = ctx.enter_context(nc.semaphore("bar"))
            block = ctx.enter_context(nc.Block())

            per_eng = {e: [] for e in ENGINES}
            for o in self.ops:
                if isinstance(o, Barrier):
                    for e in ENGINES:
                        per_eng[e].append(o)
                else:
                    per_eng[o.eng].append(o)

            def make_stream(e):
                def stream(eng):
                    nbar = 0
                    drained = [0] * NDMASEM
                    for o in per_eng[e]:
                        if isinstance(o, Barrier):
                            nbar += 1
                            cnts = o.dma_snapshot.get(e)
                            for slot in range(NDMASEM):
                                if cnts[slot] > drained[slot]:
                                    eng.wait_ge(sems[f"D_{e}{slot}"],
                                                16 * cnts[slot])
                                    drained[slot] = cnts[slot]
                            eng.drain().then_inc(barsem, 1)
                            eng.wait_ge(barsem, len(ENGINES) * nbar)
                            continue
                        if o.throttle is not None:
                            eng.wait_ge(sems[o.throttle[0]], o.throttle[1])
                        for s, v in o.waits:
                            eng.wait_ge(sems[s], v)
                        ins = o.fn(eng)
                        if o.inc_val:
                            if ins is None:
                                ins = eng.nop()
                            ins.then_inc(sems[o.sem_name], o.inc_val)
                    return None
                return stream

            for e in ENGINES:
                getattr(block, e)(make_stream(e))
        return



def np_bf16(x):
    return np.asarray(x, dtype=ml_dtypes.bfloat16)


def make_identity(n=128):
    return np_bf16(np.eye(n))


def rotate_half_matrix(dim_head, n=128):
    h = dim_head
    M = np.zeros((n, n), np.float32)
    for base in range(0, n, h):
        for d in range(h // 2):
            M[base + d, base + d + h // 2] = -1.0
            M[base + d + h // 2, base + d] = 1.0
    return np_bf16(M.T)



class Env:
    def __init__(self, nc, B, ctx):
        self.nc = nc
        self.B = B
        self.ctx = ctx
        self.psf = [ctx.enter_context(nc.psum_tensor(f"psf{i}", [128, 512], F32))
                    for i in range(6)]
        self.psb = [ctx.enter_context(nc.psum_tensor(f"psb{i}", [128, 1024], BF16))
                    for i in range(2)]
        self.psf_i = 0
        self.psb_i = 0
        self.rr_i = 0
        self.consts = {}
        self.wslot = 0
        self.nwslots = 2
        self.warena = ctx.enter_context(
            nc.sbuf_tensor("warena", [128, self.nwslots, 8192], BF16))
        self._uniq = 0

    def ps(self):
        s = getattr(self, "psf_set", None)
        if s is not None:
            i = s[self.psf_i % len(s)]
        else:
            i = self.psf_i % getattr(self, "psf_n", 6)
        self.psf_i += 1
        return self.psf[i], ("psf", i)

    def pst(self):
        s = getattr(self, "psb_set", None)
        if s is not None:
            i = s[self.psb_i % len(s)]
        else:
            i = self.psb_i % 2
        self.psb_i += 1
        return self.psb[i], ("psb", i)

    def rr(self):
        self.rr_i += 1
        return "vector" if self.rr_i % 2 else "scalar"

    def sb(self, name, shape, dt=F32, ctx=None, side=None):
        self._uniq += 1
        return (ctx or self.ctx).enter_context(
            self.nc.sbuf_tensor(f"{name}_{self._uniq}", shape, dt, side=side))

    def wtile(self, KC, slab):
        assert KC * slab <= 8192
        slot = self.wslot % self.nwslots
        self.wslot += 1
        view = self.warena[:, slot, 0:KC * slab].rearrange(
            "p (k n) -> p k n", k=KC)
        return view, ("WA", slot)


def mm_group(B, ps_ap, ps_key, parts, read_keys, tag=""):
    def fn(t, parts=parts, ps_ap=ps_ap):
        ins = None
        n = len(parts)
        for i, (l, r) in enumerate(parts):
            ins = t.matmul(ps_ap, l, r, start=(i == 0), stop=(i == n - 1))
        return ins
    return B.t(fn, reads=tuple(read_keys), writes=(ps_key,), tag=tag)


def dma_weight(B, wbuf, W_slice, key, tag=""):
    B.dma("gpsimd", wbuf[:, :, :], W_slice, reads=(), writes=(key,), tag=tag)


def stream_project_fm(B, E, ctx, W_dram, Din, Dout, src3, src_key_fn, epilogue,
                      T=2048, tt_size=512, slab=512, nbuf=2, tag="w",
                      oc_sizes=None):
    KC = Din // 128
    nslab = (Dout + slab - 1) // slab
    tiles = [E.wtile(KC, slab) for _ in range(nslab)]
    ntt = T // tt_size

    def emit_dma(s):
        w = min(slab, Dout - s * slab)
        src = W_dram[:, s * slab:s * slab + w].rearrange("(k p) n -> p k n", p=128)
        B.dma("gpsimd", tiles[s][0][:, :, 0:w], src,
              writes=(tiles[s][1],), tag=f"{tag}_dma{s}")

    look = min(E.nwslots, nslab)
    for s in range(look):
        emit_dma(s)
    for s in range(nslab):
        w = min(slab, Dout - s * slab)
        wb, wkey = tiles[s]
        for oci in range((w + 127) // 128):
            ow = min(128, w - oci * 128)
            oc = s * (slab // 128) + oci
            for tt in range(ntt):
                ps, pk = E.ps()
                parts = [(wb[:, kc, oci * 128:oci * 128 + ow],
                          src3[:, kc, tt * tt_size:(tt + 1) * tt_size])
                         for kc in range(KC)]
                rk = [wkey] + [src_key_fn(kc, tt) for kc in range(KC)]
                mm_group(B, ps[0:ow, 0:tt_size], pk, parts, rk, tag=f"{tag}mm{oc},{tt}")
                epilogue(oc, tt, ps[0:ow, 0:tt_size], pk, ow)
        if s + look < nslab:
            emit_dma(s + look)


def stream_project_tm(B, E, ctx, W_dram, Din, Dout, src3, src_key_fn, epilogue,
                      T=2048, slab=512, nbuf=2, tag="w"):
    KC = Din // 128
    nslab = (Dout + slab - 1) // slab
    tiles = [E.wtile(KC, slab) for _ in range(nslab)]
    TC = T // 128

    def emit_dma(s):
        w = min(slab, Dout - s * slab)
        src = W_dram[:, s * slab:s * slab + w].rearrange("(k p) n -> p k n", p=128)
        B.dma("gpsimd", tiles[s][0][:, :, 0:w], src,
              writes=(tiles[s][1],), tag=f"{tag}_dma{s}")

    look = min(E.nwslots, nslab)
    for s in range(look):
        emit_dma(s)
    for s in range(nslab):
        w = min(slab, Dout - s * slab)
        wb, wkey = tiles[s]
        for tc in range(TC):
            ps, pk = E.ps()
            parts = [(src3[:, kc, tc * 128:(tc + 1) * 128], wb[:, kc, 0:w])
                     for kc in range(KC)]
            rk = [wkey] + [src_key_fn(kc, tc) for kc in range(KC)]
            mm_group(B, ps[:, 0:w], pk, parts, rk, tag=f"{tag}tm{tc},{s}")
            epilogue(tc, s, ps[:, 0:w], pk, w)
        if s + look < nslab:
            emit_dma(s + look)


def ecopy(e, d, s):
    if hasattr(e, "tensor_copy"):
        return e.tensor_copy(d, s)
    return e.copy(d, s)


def emit_ln(B, E, src3, src_key, dst3, dst_key, KC, T, kind="ln",
            g_col=None, b_col=None, g_key=None, eps=1e-5, scr=None,
            src_is_f32=True, tt_size=512, tag="ln", dst_off=0):
    ones = E.consts["ones"]
    inplace = dst3 is src3
    if inplace and not src_is_f32:
        inplace = False
    D = KC * 128
    ntt = T // tt_size
    rows = scr["rows"]
    rows_bf = scr["rows_bf"]
    bc = scr["bc"]
    for tt in range(ntt):
        t0 = tt * tt_size
        sl = slice(t0, t0 + tt_size)
        dsl = slice(dst_off + t0, dst_off + t0 + tt_size)
        ps_s, pk_s = E.ps()
        ps_q, pk_q = E.ps()
        for kc in range(KC):
            if inplace:
                xb_t = scr["xb"]
                B.op(E.rr(), (lambda e, d=xb_t[:, kc % 2, :], s=src3[:, kc, sl]:
                              ecopy(e, d, s)),
                     reads=(src_key(kc, tt),), writes=(("lnxb", kc % 2),), tag=f"{tag}cast")
                xb = xb_t[:, kc % 2, :]
                xb_key = ("lnxb", kc % 2)
            elif src_is_f32:
                B.op(E.rr(), (lambda e, d=dst3[:, kc, dsl], s=src3[:, kc, sl]:
                              ecopy(e, d, s)),
                     reads=(src_key(kc, tt),), writes=((dst_key, kc, tt),), tag=f"{tag}cast")
                xb = dst3[:, kc, dsl]
                xb_key = (dst_key, kc, tt)
            else:
                xb = src3[:, kc, sl]
                xb_key = src_key(kc, tt)
            sq = scr["sq"]
            B.s(lambda e, d=sq[:, kc % 2, :], s=xb: e.activation(d, s, AF.Square),
                reads=(xb_key,), writes=(("lnsq", kc % 2),), tag=f"{tag}sq")
            if kind == "ln":
                B.t(lambda t, ps=ps_s[0:1, 0:tt_size], o=ones[:, 0:1], x=xb,
                    st=(kc == 0), sp=(kc == KC - 1):
                    t.matmul(ps, o, x, start=st, stop=sp),
                    reads=(xb_key,), writes=(pk_s,), tag=f"{tag}summ")
            B.t(lambda t, ps=ps_q[0:1, 0:tt_size], o=ones[:, 0:1],
                x=sq[:, kc % 2, 0:tt_size],
                st=(kc == 0), sp=(kc == KC - 1):
                t.matmul(ps, o, x, start=st, stop=sp),
                reads=(("lnsq", kc % 2),), writes=(pk_q,), tag=f"{tag}sqm")
        r_m = rows[0:1, 0:tt_size]
        r_rstd = rows[0:1, tt_size:2 * tt_size]
        r_mr = rows[0:1, 2 * tt_size:3 * tt_size]
        if kind == "ln":
            B.v(lambda e, d=r_m, s=ps_s[0:1, 0:tt_size]:
                e.tensor_scalar(d, s, 1.0 / D, None, ALU.mult),
                reads=(pk_s,), writes=(("lnr", 0),), tag=f"{tag}m")
            B.v(lambda e, d=r_rstd, s=ps_q[0:1, 0:tt_size]:
                e.tensor_scalar(d, s, 1.0 / D, None, ALU.mult),
                reads=(pk_q,), writes=(("lnr", 1),), tag=f"{tag}v1")
            B.v(lambda e, d=r_mr, m=r_m: e.tensor_mul(d, m, m),
                reads=(("lnr", 0),), writes=(("lnr", 2),), tag=f"{tag}msq")
            B.v(lambda e, d=r_rstd, a=r_mr, b=r_rstd: e.tensor_sub(d, a, b),
                reads=(("lnr", 1), ("lnr", 2)), writes=(("lnr", 1),), tag=f"{tag}v2")
        else:
            B.v(lambda e, d=r_rstd, s=ps_q[0:1, 0:tt_size]:
                e.tensor_scalar(d, s, -1.0 / D, None, ALU.mult),
                reads=(pk_q,), writes=(("lnr", 1),), tag=f"{tag}v1")
        B.s(lambda e, d=r_rstd, s=r_rstd, ba=E.scal(eps):
            e.activation(d, s, AF.Ln, bias=ba, scale=-1.0),
            reads=(("lnr", 1),), writes=(("lnr", 1),), tag=f"{tag}lnv")
        B.s(lambda e, d=rows_bf[0:1, 0:tt_size], s=r_rstd:
            e.activation(d, s, AF.Exp, scale=-0.5),
            reads=(("lnr", 1),), writes=(("lnrb", 0),), tag=f"{tag}cb1")
        if kind == "ln":
            B.s(lambda e, d=r_rstd, s=r_rstd: e.activation(d, s, AF.Exp,
                                                           scale=-0.5),
                reads=(("lnr", 1),), writes=(("lnr", 1),), tag=f"{tag}rstdf")
        if kind == "ln":
            B.v(lambda e, d=r_mr, m=r_m, r=r_rstd: e.tensor_mul(d, m, r),
                reads=(("lnr", 0), ("lnr", 1)), writes=(("lnr", 2),), tag=f"{tag}mr")
            B.v(lambda e, d=rows_bf[0:1, tt_size:2 * tt_size], s=r_mr: e.tensor_copy(d, s),
                reads=(("lnr", 2),), writes=(("lnrb", 1),), tag=f"{tag}cb2")
        ps_b1, pk_b1 = E.ps()
        B.t(lambda t, ps=ps_b1[:, 0:tt_size], o=ones[0:1, :], r=rows_bf[0:1, 0:tt_size]:
            t.matmul(ps, o, r, start=True, stop=True),
            reads=(("lnrb", 0),), writes=(pk_b1,), tag=f"{tag}bc1")
        B.v(lambda e, d=bc[:, 0, 0:tt_size], s=ps_b1[:, 0:tt_size]: e.tensor_copy(d, s),
            reads=(pk_b1,), writes=(("lnbc", 0),), tag=f"{tag}bce1")
        if kind == "ln":
            ps_b2, pk_b2 = E.ps()
            B.t(lambda t, ps=ps_b2[:, 0:tt_size], o=ones[0:1, :], r=rows_bf[0:1, tt_size:2 * tt_size]:
                t.matmul(ps, o, r, start=True, stop=True),
                reads=(("lnrb", 1),), writes=(pk_b2,), tag=f"{tag}bc2")
            B.s(lambda e, d=bc[:, 1, 0:tt_size], s=ps_b2[:, 0:tt_size]: e.copy(d, s),
                reads=(pk_b2,), writes=(("lnbc", 1),), tag=f"{tag}bce2")
        for kc in range(KC):
            if inplace:
                x_ap = src3[:, kc, sl]
                x_key = src_key(kc, tt)
            elif src_is_f32:
                x_ap = dst3[:, kc, dsl]
                x_key = (dst_key, kc, tt)
            else:
                x_ap = src3[:, kc, sl]
                x_key = src_key(kc, tt)
            d_ap = dst3[:, kc, dsl]
            B.v(lambda e, d=d_ap, x=x_ap, r=bc[:, 0, 0:tt_size]: e.tensor_mul(d, x, r),
                reads=(x_key, ("lnbc", 0)), writes=((dst_key, kc, tt),), tag=f"{tag}ap1")
            if kind == "ln":
                B.v(lambda e, d=d_ap, m=bc[:, 1, 0:tt_size]: e.tensor_sub(d, d, m),
                    reads=((dst_key, kc, tt), ("lnbc", 1)),
                    writes=((dst_key, kc, tt),), tag=f"{tag}ap2")
            if g_col is not None:
                bcol = b_col[:, kc:kc + 1] if b_col is not None else None
                op1 = ALU.add if b_col is not None else ALU.bypass
                B.v(lambda e, d=d_ap, g=g_col[:, kc:kc + 1], bb=bcol, o1=op1:
                    e.tensor_scalar(d, d, g, bb, ALU.mult, o1),
                    reads=((dst_key, kc, tt), g_key),
                    writes=((dst_key, kc, tt),), tag=f"{tag}gb")


def emit_transpose_grid(B, E, src_fn, src_key_fn, dst_fn, dst_key_fn, n_i, n_j,
                        tag="tr"):
    idn = E.consts["id"]
    for i in range(n_i):
        for j in range(n_j):
            pst, pk = E.pst()
            B.t(lambda t, o=pst[:, 0:128], s=src_fn(i, j), idn=idn:
                t.transpose(o, s, idn),
                reads=(src_key_fn(i, j),), writes=(pk,), tag=f"{tag}t{i},{j}")
            B.op(E.rr(), (lambda e, d=dst_fn(i, j), s=pst[:, 0:128]: ecopy(e, d, s)),
                 reads=(pk,), writes=(dst_key_fn(i, j),), tag=f"{tag}e{i},{j}")


def softmax_fm_free(B, E, buf3, keys_in_fn, key_out_fn, KC, T, scr_cols, tag="smf"):
    negmax = scr_cols[:, 0:1]
    ssum = scr_cols[:, 1:2]
    for kc in range(KC):
        ko = key_out_fn(kc)
        B.v(lambda e, d=negmax, s=buf3[:, kc, :]:
            e.reduce_max(d, s, axis=AX.X, negate=True),
            reads=tuple(keys_in_fn(kc)), writes=(("smx", 0),), tag=f"{tag}max{kc}")
        B.s(lambda e, d=buf3[:, kc, :], s=buf3[:, kc, :], b=negmax, a=ssum:
            e.activation(d, s, AF.Exp, bias=b, scale=1.0, accum_out=a),
            reads=tuple(keys_in_fn(kc)) + (("smx", 0),), writes=(ko, ("smx", 1)),
            tag=f"{tag}exp{kc}")
        B.v(lambda e, d=ssum, s=ssum: e.reciprocal(d, s),
            reads=(("smx", 1),), writes=(("smx", 1),), tag=f"{tag}rcp{kc}")
        B.v(lambda e, d=buf3[:, kc, :], r=ssum:
            e.tensor_scalar(d, d, r, None, ALU.mult),
            reads=(ko, ("smx", 1)), writes=(ko,), tag=f"{tag}mul{kc}")


def softmax_tm_heads(B, E, buf3, keys_in_fn, key_out_fn, TC, nh, dh, scr,
                     tag="smt"):
    negmax = scr[:, 0:nh]
    sums = scr[:, nh:2 * nh]
    for tc in range(TC):
        ko = key_out_fn(tc)
        x3 = buf3[:, tc, :].rearrange("p (h d) -> p h d", h=nh)
        B.v(lambda e, d=negmax, s=x3: e.reduce_max(d, s, axis=AX.X, negate=True),
            reads=tuple(keys_in_fn(tc)), writes=(("smtx", 0),), tag=f"{tag}max{tc}")
        for h in range(nh):
            B.s(lambda e, d=buf3[:, tc, h * dh:(h + 1) * dh],
                b=negmax[:, h:h + 1], a=sums[:, h:h + 1]:
                e.activation(d, d, AF.Exp, bias=b, scale=1.0, accum_out=a),
                reads=tuple(keys_in_fn(tc)) + (("smtx", 0),),
                writes=(ko, ("smtx", 1)),
                tag=f"{tag}exp{tc},{h}")
        B.v(lambda e, d=sums, s=sums: e.reciprocal(d, s),
            reads=(("smtx", 1),), writes=(("smtx", 1),), tag=f"{tag}rcp{tc}")
        for h in range(nh):
            B.v(lambda e, d=buf3[:, tc, h * dh:(h + 1) * dh], r=sums[:, h:h + 1]:
                e.tensor_scalar(d, d, r, None, ALU.mult),
                reads=(ko, ("smtx", 1)), writes=(ko,),
                tag=f"{tag}mul{tc},{h}")


def load_vec_cols(B, E, vecs, stage, cols, cols_key, tag="vc"):
    r0 = 0
    spans = []
    for vap in vecs:
        D = vap.shape[0]
        if D == 8:
            B.dma("sync", stage[r0:r0 + 1, 0:8], vap[None, :],
                  writes=(("vstage",),), tag=f"{tag}dma")
            spans.append((r0, 1))
            r0 += 1
            continue
        if D == 64:
            B.dma("sync", stage[r0:r0 + 1, 0:64], vap[None, :],
                  writes=(("vstage",),), tag=f"{tag}dma")
            B.dma("sync", stage[r0:r0 + 1, 64:128], vap[None, :],
                  writes=(("vstage",),), tag=f"{tag}dma")
            spans.append((r0, 1))
            r0 += 1
            continue
        kc = D // 128
        B.dma("sync", stage[r0:r0 + kc, :], vap.rearrange("(k p) -> k p", p=128),
              writes=(("vstage",),), tag=f"{tag}dma")
        spans.append((r0, kc))
        r0 += kc
    R = r0
    idf = E.consts["idf"]
    ps, pk = E.ps()
    B.t(lambda t, o=ps[:, 0:R], s=stage[0:R, :], i=idf[0:R, 0:R]:
        t.transpose(o, s, i),
        reads=(("vstage",),), writes=(pk,), tag=f"{tag}tr")
    B.v(lambda e, d=cols[:, 0:R], s=ps[:, 0:R]: e.tensor_copy(d, s),
        reads=(pk,), writes=(cols_key,), tag=f"{tag}ev")
    return spans


def emit_residual_finale(B, E, xs, xs_key_fn, src_dram, dst_dram, scr2, KC=8,
                         T=2048, tt_size=512, tag="fin"):
    ntt = T // tt_size
    i = 0
    for kc in range(KC):
        for tt in range(ntt):
            sl = slice(tt * tt_size, (tt + 1) * tt_size)
            par = i % 2
            i += 1
            B.dma("sync", scr2[:, par, 0:tt_size],
                  src_dram[kc * 128:(kc + 1) * 128, sl],
                  writes=(("fins", par),), tag=f"{tag}ld")
            B.v(lambda e, d=xs[:, kc, sl], s=scr2[:, par, 0:tt_size]:
                e.tensor_add(d, d, s),
                reads=(("fins", par), xs_key_fn(kc, tt)),
                writes=(xs_key_fn(kc, tt),), tag=f"{tag}add")
            B.dma("scalar", dst_dram[kc * 128:(kc + 1) * 128, sl], xs[:, kc, sl],
                  reads=(xs_key_fn(kc, tt),), writes=(("find", kc, tt),),
                  tag=f"{tag}st")


def load_fm_f32(B, E, dst3, dst_key, src_dram, KC=8, T=2048, tag="ld"):
    for kc in range(KC):
        B.dma("sync", dst3[:, kc, :], src_dram[kc * 128:(kc + 1) * 128, :],
              writes=((dst_key, kc, 0), (dst_key, kc, 1), (dst_key, kc, 2),
                      (dst_key, kc, 3)),
              tag=f"{tag}{kc}")



L1A_DEPTH = 6


def phase_l1a(B, E, d, src_dram, dst_dram):
    nc = E.nc
    KC, T, DEPTH, NH, DH = 8, 2048, L1A_DEPTH, 8, 128
    TC = T // 128
    with ExitStack() as pctx:
        xs = E.sb("l1a_xs", [128, KC, T], F32, pctx)
        xn = E.sb("l1a_xn", [128, KC, T], BF16, pctx)
        scr = {
            "xb": E.sb("l1a_sxb", [128, 2, 512], BF16, pctx),
            "sq": E.sb("l1a_ssq", [128, 2, 512], BF16, pctx),
            "rows": E.sb("l1a_srows", [1, 1536], F32, pctx),
            "rows_bf": E.sb("l1a_srbf", [1, 1024], BF16, pctx),
            "bc": E.sb("l1a_sbc", [128, 2, 512], BF16, pctx),
        }
        cols = E.sb("l1a_cols", [128, 48], F32, pctx)
        vstage = E.sb("l1a_vstage", [48, 128], F32, pctx)
        ctxb = E.sb("l1a_ctx", [128, NH * 128], BF16, pctx)
        smcols = E.sb("l1a_smc", [128, 80], F32, pctx)
        ktmp = E.sb("l1a_ktmp", [128, 2, 128], BF16, pctx)
        idn = E.consts["id"]

        xsk = lambda kc, tt: ("xs", kc, tt)
        xnk = lambda kc, tt: ("xn", kc, tt)

        import os as _os2
        load_fm_f32(B, E, xs, "xs", src_dram)
        E.dump("x0", xs[:, :, :].rearrange("p a b -> p (a b)"))
        if _os2.environ.get("NOLN", "0") != "1":
            emit_ln(B, E, xs, lambda kc, tt: ("xs", kc, tt), xs, "xs", KC, T,
                    kind="ln", eps=1e-5, scr=scr, src_is_f32=True, tag="ln0")
        E.dump("xln0", xs[:, :, :].rearrange("p a b -> p (a b)"))
        B.barrier()

        for li in range(DEPTH):
            load_vec_cols(B, E, [d["L1_attn_ln1_w"][li], d["L1_attn_ln1_b"][li],
                                 d["L1_attn_ln2_w"][li], d["L1_attn_ln2_b"][li],
                                 d["L1_attn_bo"][li]],
                          vstage, cols, ("cols",), tag=f"l1a_vc{li}")
            emit_ln(B, E, xs, xsk, xn, "xn", KC, T, kind="ln",
                    g_col=cols[:, 0:8], b_col=cols[:, 8:16], g_key=("cols",),
                    eps=1e-5, scr=scr, src_is_f32=True, tag=f"l1a_ln1_{li}")

            if li == DEBUG_LI:
                E.dump("xn1", xn[:, :, :].rearrange("p a b -> p (a b)"))
            with ExitStack() as c_kv:
                kf = E.sb(f"l1a_kf{li}", [128, KC, T], BF16, c_kv)
                vtmp = E.sb(f"l1a_vtmp{li}", [128, 2, 512], BF16, c_kv)

                def ep_kf(oc, tt, ps, pk, ow):
                    B.op(E.rr(), (lambda e, dd=kf[:, oc, tt * 512:(tt + 1) * 512],
                                  s=ps: ecopy(e, dd, s)),
                         reads=(pk,), writes=(("kf", oc, tt),), tag="epkf")
                    if tt < 3:
                        return
                    negmax = smcols[:, oc:oc + 1]
                    ssum = smcols[:, 8 + oc:9 + oc]
                    rks = tuple(("kf", oc, t4) for t4 in range(4))
                    B.v(lambda e, dd=negmax, s=kf[:, oc, :]:
                        e.reduce_max(dd, s, axis=AX.X, negate=True),
                        reads=rks, writes=(("smx", oc, 0),), tag=f"smk_max{oc}")
                    B.s(lambda e, dd=kf[:, oc, :], s=kf[:, oc, :], b=negmax,
                        a=ssum: e.activation(dd, s, AF.Exp, bias=b, scale=1.0,
                                             accum_out=a),
                        reads=rks + (("smx", oc, 0),),
                        writes=(("kfs", oc), ("smx", oc, 1)), tag=f"smk_exp{oc}")
                    B.v(lambda e, dd=ssum, s=ssum: e.reciprocal(dd, s),
                        reads=(("smx", oc, 1),), writes=(("smx", oc, 1),),
                        tag=f"smk_rcp{oc}")
                    B.v(lambda e, dd=kf[:, oc, :], r=ssum:
                        e.tensor_scalar(dd, dd, r, None, ALU.mult),
                        reads=(("kfs", oc), ("smx", oc, 1)),
                        writes=(("kfs", oc),), tag=f"smk_mul{oc}")
                stream_project_fm(B, E, None, d["L1_attn_wk"][li], 1024, 1024,
                                  xn, xnk, ep_kf, tag=f"wk{li}")
                if li == DEBUG_LI:
                    E.dump("kraw", kf[:, :, :].rearrange("p a b -> p (a b)"))

                E.psf_n = 2
                ctxps = [E.psf[2 + j] for j in range(4)]
                ctxpk = [("psf", 2 + j) for j in range(4)]

                def ep_vt(tc, fs, ps, pk, fw):
                    par = tc % 2
                    B.op(E.rr(), (lambda e, dd=vtmp[:, par, :], s=ps:
                                  ecopy(e, dd, s)),
                         reads=(pk,), writes=(("vtmp", par),), tag="epvt")
                    for j in range(4):
                        h = fs * 4 + j
                        pst, pkt = E.pst()
                        B.t(lambda t, o=pst[:, 0:128],
                            s=kf[:, h, tc * 128:(tc + 1) * 128], i=idn:
                            t.transpose(o, s, i),
                            reads=(("kfs", h),), writes=(pkt,), tag=f"trk{h},{tc}")
                        B.v(lambda e, dd=ktmp[:, (tc * 4 + j) % 2, :],
                            s=pst[:, 0:128]: e.tensor_copy(dd, s),
                            reads=(pkt,), writes=(("ktmp", (tc * 4 + j) % 2),),
                            tag="trke")
                        B.t(lambda t, ps2=ctxps[j][:, fs * 128:(fs + 1) * 128],
                            l=ktmp[:, (tc * 4 + j) % 2, :],
                            r=vtmp[:, par, j * 128:(j + 1) * 128],
                            st=(tc == 0), sp=(tc == TC - 1):
                            t.matmul(ps2, l, r, start=st, stop=sp),
                            reads=(("ktmp", (tc * 4 + j) % 2), ("vtmp", par)),
                            writes=(ctxpk[j],), tag=f"ctxmm{h},{tc}")
                stream_project_tm(B, E, None, d["L1_attn_wv"][li], 1024, 1024,
                                  xn, xnk, ep_vt, tag=f"wv{li}")
                for fs in range(2):
                    for j in range(4):
                        h = fs * 4 + j
                        B.op(E.rr(), (lambda e, dd=ctxb[:, h * 128:(h + 1) * 128],
                                      s=ctxps[j][:, fs * 128:(fs + 1) * 128]:
                                      ecopy(e, dd, s)),
                             reads=(ctxpk[j],), writes=(("ctx", h),),
                             tag=f"ctxev{h}")
                E.psf_n = 6
                B.barrier()
                if li == DEBUG_LI:
                    E.dump("kf", kf[:, :, :].rearrange("p a b -> p (a b)"))
                    E.dump("ctx", ctxb[:, :])

            with ExitStack() as c_q:
                qt = E.sb(f"l1a_qt{li}", [128, TC, 1024], BF16, c_q)
                qstage = E.sb(f"l1a_qst{li}", [128, 2, 512], BF16, c_q)

                def ep_qt(tc, fs, ps, pk, fw):
                    B.op(E.rr(), (lambda e, dd=qt[:, tc, fs * 512:fs * 512 + fw],
                                  s=ps: ecopy(e, dd, s)),
                         reads=(pk,), writes=(("qt", tc, fs),), tag="epqt")
                    if fs != 1:
                        return
                    sr = tc % 4
                    sl0 = 16 + sr * 16
                    negmax = smcols[:, sl0:sl0 + NH]
                    sums = smcols[:, sl0 + NH:sl0 + 2 * NH]
                    rks = (("qt", tc, 0), ("qt", tc, 1))
                    x3 = qt[:, tc, :].rearrange("p (h d) -> p h d", h=NH)
                    B.v(lambda e, dd=negmax, s=x3:
                        e.reduce_max(dd, s, axis=AX.X, negate=True),
                        reads=rks, writes=(("smtx", sr, 0),), tag=f"smq_max{tc}")
                    for h in range(NH):
                        B.s(lambda e, dd=qt[:, tc, h * DH:(h + 1) * DH],
                            b=negmax[:, h:h + 1], a=sums[:, h:h + 1]:
                            e.activation(dd, dd, AF.Exp, bias=b, scale=1.0,
                                         accum_out=a),
                            reads=rks + (("smtx", sr, 0),),
                            writes=(("qts", tc), ("smtx", sr, 1)),
                            tag=f"smq_exp{tc},{h}")
                    B.v(lambda e, dd=sums, s=sums: e.reciprocal(dd, s),
                        reads=(("smtx", sr, 1),), writes=(("smtx", sr, 1),),
                        tag=f"smq_rcp{tc}")
                    for h in range(NH):
                        B.v(lambda e, dd=qt[:, tc, h * DH:(h + 1) * DH],
                            r=sums[:, h:h + 1]:
                            e.tensor_scalar(dd, dd, r, None, ALU.mult),
                            reads=(("qts", tc), ("smtx", sr, 1)),
                            writes=(("qts", tc),), tag=f"smq_mul{tc},{h}")
                stream_project_tm(B, E, None, d["L1_attn_wq"][li], 1024, 1024,
                                  xn, xnk, ep_qt, tag=f"wq{li}")
                def attn_body(h, tt):
                    par = tt % 2
                    for j in range(4):
                        tc = tt * 4 + j
                        pst, pkt = E.pst()
                        B.t(lambda t, o=pst[:, 0:128],
                            s=qt[:, tc, h * 128:(h + 1) * 128], i=idn:
                            t.transpose(o, s, i),
                            reads=(("qts", tc),),
                            writes=(pkt,), tag=f"trq{h},{tc}")
                        B.v(lambda e, dd=qstage[:, par, j * 128:(j + 1) * 128],
                            s=pst[:, 0:128]: e.tensor_copy(dd, s),
                            reads=(pkt,), writes=(("qstg", par, j),), tag="trqe")
                    ps, pk = E.ps()
                    parts = [(ctxb[:, h * 128:(h + 1) * 128],
                              qstage[:, par, 0:512])]
                    rk = [("ctx", h)] + [("qstg", par, j) for j in range(4)]
                    mm_group(B, ps[:, 0:512], pk, parts, rk, tag=f"att{h},{tt}")
                    B.op(E.rr(), (lambda e, dd=xn[:, h, tt * 512:(tt + 1) * 512],
                                  s=ps[:, 0:512]: ecopy(e, dd, s)),
                         reads=(pk,), writes=(("xn", h, tt),), tag="attev")
                for h in range(NH):
                    for tp in (0, 2):
                        B.braid([lambda t2=tp: attn_body(h, t2),
                                 lambda t2=tp + 1: attn_body(h, t2)])
                if li == DEBUG_LI:
                    E.dump("qt", qt[:, :, :].rearrange("p a b -> p (a b)"))
                    E.dump("attn", xn[:, :, :].rearrange("p a b -> p (a b)"))
                B.barrier()

            def ep_wo(oc, tt, ps, pk, ow):
                B.v(lambda e, dd=xs[:, oc, tt * 512:(tt + 1) * 512], s=ps,
                    bo=cols[:, 32 + oc:32 + oc + 1]:
                    e.scalar_tensor_tensor(dd, s, bo, dd, ALU.add, ALU.add),
                    reads=(pk, ("cols",), ("xs", oc, tt)),
                    writes=(("xs", oc, tt),), tag="epwo")
            stream_project_fm(B, E, None, d["L1_attn_wo"][li], 1024, 1024,
                              xn, xnk, ep_wo, tag=f"wo{li}")
            B.barrier()

            if li == DEBUG_LI:
                E.dump("xswo", xs[:, :, :].rearrange("p a b -> p (a b)"))
            emit_ln(B, E, xs, xsk, xn, "xn", KC, T, kind="ln",
                    g_col=cols[:, 16:24], b_col=cols[:, 24:32], g_key=("cols",),
                    eps=1e-5, scr=scr, src_is_f32=True, tag=f"l1a_ln2_{li}")

            with ExitStack() as c_h:
                h1 = E.sb(f"l1a_h1_{li}", [128, 8, T], BF16, c_h)
                for qtr in range(4):
                    def ep_h1(oc, tt, ps, pk, ow):
                        sl5 = slice(tt * 512, (tt + 1) * 512)
                        B.op(E.rr(), (lambda e, dd=h1[:, oc, sl5], s=ps:
                                      ecopy(e, dd, s)),
                             reads=(pk,), writes=(("h1", oc, tt),), tag="eph1c")
                        B.v(lambda e, dd=h1[:, oc, sl5]:
                            e.scalar_tensor_tensor(dd, dd, 0.01, dd,
                                                   ALU.mult, ALU.max),
                            reads=(("h1", oc, tt),), writes=(("h1", oc, tt),),
                            tag="eph1l")
                    stream_project_fm(B, E, None,
                                      d["L1_attn_w1"][li][:, qtr * 1024:(qtr + 1) * 1024],
                                      1024, 1024, xn, xnk, ep_h1, tag=f"w1_{li}{qtr}")
                    w2d = d["L1_attn_w2"][li][qtr * 1024:(qtr + 1) * 1024, :]
                    w2tiles = [E.wtile(8, 512) for _ in range(2)]
                    for s2 in range(2):
                        B.dma("gpsimd", w2tiles[s2][0][:, :, :],
                              w2d[:, s2 * 512:(s2 + 1) * 512]
                              .rearrange("(k p) n -> p k n", p=128),
                              writes=(w2tiles[s2][1],), tag=f"w2dma{li}{qtr}{s2}")
                    for s2 in range(2):
                        w2b, w2k = w2tiles[s2]
                        for oci in range(4):
                            oc = s2 * 4 + oci
                            for tt in range(4):
                                ps, pk = E.ps()
                                parts = [(w2b[:, kc, oci * 128:(oci + 1) * 128],
                                          h1[:, kc, tt * 512:(tt + 1) * 512])
                                         for kc in range(8)]
                                rk = [w2k] + [("h1", kc, tt) for kc in range(8)]
                                mm_group(B, ps[:, 0:512], pk, parts, rk,
                                         tag=f"w2mm{li}{qtr}{oc}{tt}")
                                B.v(lambda e, dd=xs[:, oc, tt * 512:(tt + 1) * 512],
                                    s=ps[:, 0:512]: e.tensor_add(dd, dd, s),
                                    reads=(pk, ("xs", oc, tt)),
                                    writes=(("xs", oc, tt),), tag="epw2")
                B.barrier()
            import os as _os3
            if li == int(_os3.environ.get("DBGLI", "-1")):
                E.dump("xsl", xs[:, :, :].rearrange("p a b -> p (a b)"))
        with ExitStack() as c_f:
            fin2 = E.sb("l1a_fin", [128, 2, 512], F32, c_f)
            emit_residual_finale(B, E, xs, xsk, src_dram, dst_dram, fin2)
            B.barrier()



INPUT_SHAPES = {
    "L0_attn_w_ln1": [1024], "L0_attn_w_ln2": [1024],
    "L0_attn_W_qkvz": [1024, 3072], "L0_attn_W_ba": [1024, 16],
    "L0_attn_conv_w": [2048, 4], "L0_attn_dt_bias": [8], "L0_attn_A_log": [8],
    "L0_attn_gate_norm_w": [128], "L0_attn_W_out": [1024, 1024],
    "L0_attn_W_router": [1024, 8], "L0_attn_W_gate_e": [8, 1024, 512],
    "L0_attn_W_up_e": [8, 1024, 512], "L0_attn_W_down_e": [8, 512, 1024],
    "L0_ffn_Wqkv": [6, 1024, 3072], "L0_ffn_q_scale": [6, 64],
    "L0_ffn_k_scale": [6, 64], "L0_ffn_Wo": [6, 1024, 1024],
    "L0_ffn_ln_g": [6, 1024], "L0_ffn_ln_b": [6, 1024],
    "L0_ffn_W1re": [6, 1024, 5632], "L0_ffn_b1re": [6, 5632],
    "L0_ffn_W2re": [6, 2816, 1024], "L0_ffn_b2": [6, 1024],
    "L1_attn_ln1_w": [6, 1024], "L1_attn_ln1_b": [6, 1024],
    "L1_attn_wq": [6, 1024, 1024], "L1_attn_wk": [6, 1024, 1024],
    "L1_attn_wv": [6, 1024, 1024], "L1_attn_wo": [6, 1024, 1024],
    "L1_attn_bo": [6, 1024], "L1_attn_ln2_w": [6, 1024],
    "L1_attn_ln2_b": [6, 1024], "L1_attn_w1": [6, 1024, 4096],
    "L1_attn_w2": [6, 4096, 1024],
    "L1_ffn_ln1_w": [1024], "L1_ffn_ln1_b": [1024],
    "L1_ffn_ln2_w": [1024], "L1_ffn_ln2_b": [1024],
    "L1_ffn_conv_w": [1024, 1, 3], "L1_ffn_attn_w": [1024, 3072],
    "L1_ffn_proj_w": [1024, 1024], "L1_ffn_router_w": [1024, 4],
    "L1_ffn_expert_wv": [4, 1024, 8192], "L1_ffn_expert_proj": [4, 4096, 1024],
}


SCALS = [1e-5, 1e-6, 1.0, 8.0, 0.5, 2.0, 0.0, 0.01, 1e-24, -2.4260151319598084]


def make_const_arrays():
    c = {}
    c["c_ones"] = np_bf16(np.ones((128, 128)))
    c["c_id"] = make_identity(128)
    c["c_idf"] = np.eye(128, dtype=np.float32)
    c["c_scal"] = np.asarray([SCALS], np.float32)
    c["c_rot64"] = rotate_half_matrix(64, 128)
    D, T = 64, 2048
    inv = 1.0 / (10000.0 ** (np.arange(0, D, 2, dtype=np.float64) / D))
    t = np.arange(T, dtype=np.float64)
    fr = t[:, None] * inv[None, :]
    emb = np.concatenate([fr, fr], axis=1)
    cos = np.cos(emb).T
    sin = np.sin(emb).T
    c["c_l1f_rope"] = np_bf16(np.stack([np.tile(cos, (2, 1)),
                                        np.tile(sin, (2, 1))]))
    masks = np.zeros((4, 128, 512), np.float32)
    for b in range(4):
        rj = np.arange(128)[:, None]
        ri = np.arange(512)[None, :]
        masks[b] = np.where(b * 128 + rj <= ri, 0.0, -8000.0)
    c["c_l1f_causal"] = np_bf16(masks)
    DH, W, T2 = 64, 256, 2048
    half = np.arange(0, DH, 2, dtype=np.float64)
    inv_freq = 1.0 / 10000.0 ** (half / DH)
    tt2 = np.arange(T2, dtype=np.float64)
    fr2 = tt2[:, None] * inv_freq[None, :]
    emb2 = np.concatenate([fr2, fr2], 1)
    base_scale = (half + 0.4 * DH) / (1.4 * DH)
    power = (tt2 - T2 // 2) / (W // 2)
    sc = base_scale[None, :] ** power[:, None]
    sc = np.concatenate([sc, sc], 1)
    cos2, sin2 = np.cos(emb2), np.sin(emb2)
    CS = (cos2 * sc).T; SS = (sin2 * sc).T
    CSI = (cos2 / sc).T; SSI = (sin2 / sc).T
    c["c_l0f_rope"] = np_bf16(np.stack([np.tile(a, (2, 1))
                                        for a in (CS, SS, CSI, SSI)]))
    wm = np.zeros((4, 128, 256), np.float32)
    rj = np.arange(128)[:, None]
    ri = np.arange(256)[None, :]
    wm[0] = np.where(0 * 128 + rj >= ri, 0.0, -500.0)
    wm[1] = np.where(1 * 128 + rj >= ri, 0.0, -500.0)
    wm[2] = np.where(0 * 128 + rj <= ri, 0.0, -500.0)
    wm[3] = np.where(1 * 128 + rj <= ri, 0.0, -500.0)
    c["c_l0f_masks"] = np_bf16(wm)
    hm = np.zeros((128, 2), np.float32)
    hm[0:64, 0] = 1.0
    hm[64:128, 1] = 1.0
    c["c_hmask"] = np_bf16(hm)
    c["c_hmaskT"] = np_bf16(hm.T)
    s_i = np.arange(128)[:, None]
    t_i = np.arange(128)[None, :]
    dn = np.zeros((5, 128, 128), np.float32)
    dn[0] = (s_i <= t_i)
    dn[1] = np.where(s_i <= t_i, 0.0, -100.0)
    dn[2] = (s_i < t_i)
    dn[3] = (t_i < s_i)
    dn[4] = np.where(t_i < s_i, 0.0, 100.0)
    c["c_dn"] = np_bf16(dn)
    return c


def setup_program(phases, weight_names=None, extra_consts=None,
                  detect_races=True):
    nc = bass.Bass("TRN2", target_bir_lowering=False,
                   detect_race_conditions=detect_races)
    consts = make_const_arrays()
    if extra_consts:
        consts.update(extra_consts)
    d = {}
    names = weight_names if weight_names is not None else list(INPUT_SHAPES)
    for name in names:
        d[name] = nc.dram_tensor(name, INPUT_SHAPES[name], F32,
                                 kind="ExternalInput").ap()
    for name, arr in consts.items():
        dt = BF16 if arr.dtype == ml_dtypes.bfloat16 else F32
        d[name] = nc.dram_tensor(name, list(arr.shape), dt,
                                 kind="ExternalInput").ap()
    x = nc.dram_tensor("x", [1024, 2048], F32, kind="ExternalInput").ap()
    y = nc.dram_tensor("y", [1024, 2048], F32, kind="ExternalOutput").ap()
    dbg = {}
    for nm, (fr, dts) in DEBUG_BUFS.items():
        dbg[nm] = nc.dram_tensor("dbg_" + nm, [128, fr],
                                 BF16 if dts == "bf16" else F32,
                                 kind="ExternalOutput").ap()
    hs = [nc.dram_tensor(f"h{i}", [1024, 2048], F32, kind="Internal").ap()
          for i in range(max(0, len(phases) - 1))]

    B = Builder(nc)
    with ExitStack() as ctx:
        E = Env(nc, B, ctx)
        cones = E.sb("cones", [128, 128], BF16)
        cid = E.sb("cid", [128, 128], BF16)
        cidf = E.sb("cidf", [128, 128], F32)
        cscal = E.sb("cscal", [1, len(SCALS)], F32)
        E.consts["ones"] = cones[:, :]
        E.consts["id"] = cid[:, :]
        E.consts["idf"] = cidf[:, :]
        E.consts["scal"] = cscal
        E.scal = lambda v: cscal[0:1, SCALS.index(v):SCALS.index(v) + 1]
        B.dma("sync", cones[:, :], d["c_ones"], writes=(("K", "ones"),))
        B.dma("sync", cid[:, :], d["c_id"], writes=(("K", "id"),))
        B.dma("sync", cidf[:, :], d["c_idf"], writes=(("K", "idf"),))
        B.dma("sync", cscal[0:1, :], d["c_scal"], writes=(("K", "scal"),))
        B.barrier()
        E.dbg = dbg

        import os as _os
        _bar_only = _os.environ.get("DUMP_BAR_ONLY", "0") == "1"

        def dump(nm, ap):
            if nm in dbg:
                B.barrier()
                if not _bar_only:
                    n = ap.shape[-1]
                    P = ap.shape[0]
                    npc = max(1, n // 2048)
                    step = n // npc
                    for i in range(npc):
                        B.dma("gpsimd", dbg[nm][0:P, i * step:(i + 1) * step],
                              ap[:, i * step:(i + 1) * step], tag=f"dump_{nm}{i}")
                B.barrier()
        E.dump = dump
        srcs = [x] + hs
        dsts = hs + [y]
        for i, ph in enumerate(phases):
            ph(B, E, d, srcs[i], dsts[i])
        B.realize()
    return nc, consts


def host_prepare_weights(inputs_full):
    W1 = np.asarray(inputs_full["L0_ffn_W1"])
    b1 = np.asarray(inputs_full["L0_ffn_b1"])
    W2 = np.asarray(inputs_full["L0_ffn_W2"])
    IN = 2730
    PAD = 2816
    W1re = np.zeros((6, 1024, 2 * PAD), np.float32)
    W1re[:, :, 0:IN] = W1[:, :, 0:IN]
    W1re[:, :, PAD:PAD + IN] = W1[:, :, IN:2 * IN]
    b1re = np.zeros((6, 2 * PAD), np.float32)
    b1re[:, 0:IN] = b1[:, 0:IN]
    b1re[:, PAD:PAD + IN] = b1[:, IN:2 * IN]
    W2re = np.zeros((6, PAD, 1024), np.float32)
    W2re[:, 0:IN, :] = W2
    return {"L0_ffn_W1re": W1re, "L0_ffn_b1re": b1re, "L0_ffn_W2re": W2re}


def host_inputs_for(inputs_full, consts, batch_idx):
    m = {}
    derived = host_prepare_weights(inputs_full)
    for name in INPUT_SHAPES:
        if name in derived:
            m[name] = derived[name]
        else:
            m[name] = np.ascontiguousarray(inputs_full[name])
    for k, v in consts.items():
        m[k] = v
    x = inputs_full["L0_attn_hidden_states"][batch_idx]
    m["x"] = np.ascontiguousarray(np.asarray(x).T)
    return m



def phase_l1f(B, E, d, src_dram, dst_dram):
    nc = E.nc
    KC, T = 8, 2048
    NH, DH = 16, 64
    idn = E.consts["id"]
    idf = E.consts["idf"]
    ones = E.consts["ones"]
    spill = nc.dram_tensor("l1f_spill", [1024, 2048], F32, kind="Internal").ap()
    with ExitStack() as pctx:
        scr = {
            "xb": E.sb("l1f_sxb", [128, 2, 512], BF16, pctx),
            "sq": E.sb("l1f_ssq", [128, 2, 512], BF16, pctx),
            "rows": E.sb("l1f_srows", [1, 1536], F32, pctx),
            "rows_bf": E.sb("l1f_srbf", [1, 1024], BF16, pctx),
            "bc": E.sb("l1f_sbc", [128, 2, 512], BF16, pctx),
        }
        cols = E.sb("l1f_cols", [128, 48], F32, pctx)
        vstage = E.sb("l1f_vstage", [48, 128], F32, pctx)
        convw = E.sb("l1f_convw", [128, KC, 3], F32, pctx)
        srow = E.sb("l1f_srow", [1, 16], F32, pctx)
        bcol = E.sb("l1f_bcol", [128, 1], F32, pctx)
        accf = E.sb("l1f_accf", [128, 512], F32, pctx)

        xsk = lambda kc, tt: ("xs", kc, tt)

        B.dma("sync", convw[:, :, :],
              d["L1_ffn_conv_w"].rearrange("(k p) one j -> p k (one j)", p=128),
              writes=(("convw",),))
        load_vec_cols(B, E, [d["L1_ffn_ln1_w"], d["L1_ffn_ln1_b"],
                             d["L1_ffn_ln2_w"], d["L1_ffn_ln2_b"]],
                      vstage, cols, ("cols",), tag="l1f_vc")

        s_ropes = ExitStack()
        rope = E.sb("l1f_rope", [128, 2, T], BF16, s_ropes, side="right")
        cmask = E.sb("l1f_cmask", [128, 4, 512], BF16, s_ropes, side="right")
        rot64 = E.sb("l1f_rot", [128, 128], BF16, s_ropes, side="right")
        B.dma("sync", rope[:, :, :], d["c_l1f_rope"].rearrange("a p t -> p a t"),
              writes=(("rope",),))
        B.dma("sync", cmask[:, :, :], d["c_l1f_causal"].rearrange("a p t -> p a t"),
              writes=(("cmask",),))
        B.dma("sync", rot64[:, :], d["c_rot64"], writes=(("rot64",),))

        s_cv = ExitStack()
        xcv = E.sb("l1f_xcv", [128, KC, T], BF16, s_cv, side="right")
        with ExitStack() as s_xs:
            xs = E.sb("l1f_xs0", [128, KC, T], F32, s_xs, side="right")
            xcp = E.sb("l1f_xcp", [128, KC, T + 4], BF16, s_xs, side="right")
            load_fm_f32(B, E, xs, "xs", src_dram)
            emit_ln(B, E, xs, xsk, xs, "xs", KC, T, kind="ln", eps=1e-5,
                    scr=scr, src_is_f32=True, tag="l1f_ln0")
            B.v(lambda e, a=xcp[:, :, 0:2]: e.memset(a, 0.0),
                writes=(("xcpz",),), tag="l1fpz")
            emit_ln(B, E, xs, xsk, xcp, "xcp", KC, T, kind="ln",
                    g_col=cols[:, 0:8], b_col=cols[:, 8:16], g_key=("cols",),
                    eps=1e-5, scr=scr, src_is_f32=True, tag="l1f_ln1",
                    dst_off=2)
            for kc in range(KC):
                B.dma("scalar", spill[kc * 128:(kc + 1) * 128, :],
                      xs[:, kc, :],
                      reads=tuple(("xs", kc, tt) for tt in range(4)),
                      tag=f"spill{kc}")
            for kc in range(KC):
                for tt in range(4):
                    t0 = tt * 512
                    rk = (("xcp", kc, tt), ("xcpz",), ("convw",))
                    if tt > 0:
                        rk = rk + (("xcp", kc, tt - 1),)
                    B.v(lambda e, a=accf[:, 0:512], x=xcp[:, kc, t0:t0 + 512],
                        w=convw[:, kc, 0:1]:
                        e.tensor_scalar(a, x, w, None, ALU.mult),
                        reads=rk, writes=(("cacc",),), tag=f"cv0_{kc}_{tt}")
                    B.v(lambda e, a=accf[:, 0:512],
                        x=xcp[:, kc, t0 + 1:t0 + 513], w=convw[:, kc, 1:2]:
                        e.scalar_tensor_tensor(a, x, w, a, ALU.mult, ALU.add),
                        reads=rk + (("cacc",),), writes=(("cacc",),),
                        tag=f"cv1_{kc}_{tt}")
                    B.v(lambda e, o=xcv[:, kc, t0:t0 + 512],
                        x=xcp[:, kc, t0 + 2:t0 + 514], w=convw[:, kc, 2:3],
                        a=accf[:, 0:512]:
                        e.scalar_tensor_tensor(o, x, w, a, ALU.mult, ALU.add),
                        reads=rk + (("cacc",),), writes=(("xcv", kc, tt),),
                        tag=f"cv2_{kc}_{tt}")
            B.barrier()
        xcvk = lambda kc, tt: ("xcv", kc, tt)

        s_qf = ExitStack()
        qf = E.sb("l1f_qf", [128, KC, T], BF16, s_qf)
        s_kv = ExitStack()
        vt = E.sb("l1f_vt", [128, T // 128, 16 * 65], BF16, s_kv)
        attr = E.sb("l1f_attr", [128, 2, 512], F32, s_kv)
        kfb = E.sb("l1f_kf", [128, KC, T], BF16, s_kv)
        exps = E.sb("l1f_exp", [128, 4, 512], BF16, s_kv)
        rbc = E.sb("l1f_rbc", [64, 2, 512], BF16, s_kv)

        def mkep(buf, bk):
            def ep(oc, tt, ps, pk, ow):
                B.op(E.rr(), (lambda e, dd=buf[:, oc, tt * 512:(tt + 1) * 512],
                              s=ps: ecopy(e, dd, s)),
                     reads=(pk,), writes=((bk, oc, tt),), tag="ep" + bk)
            return ep

        def ep_v(tc, fs, ps, pk, fw):
            nh8 = fw // 64
            B.op(E.rr(), (lambda e,
                          dd=vt[:, tc, fs * 8 * 65:(fs * 8 + nh8) * 65]
                          .rearrange("p (h c) -> p h c", c=65)[:, :, 0:64],
                          s=ps.rearrange("p (h c) -> p h c", c=64):
                          ecopy(e, dd, s)),
                 reads=(pk,), writes=(("vt", tc, fs),), tag="epv")
            if fs == 0:
                B.v(lambda e, dd=vt[:, tc, :]
                    .rearrange("p (h c) -> p h c", c=65)[:, :, 64:65]:
                    e.memset(dd, 1.0),
                    writes=(("vt", tc, "ones"),), tag="epvone")
        aw = d["L1_ffn_attn_w"]
        stream_project_tm(B, E, None, aw[:, 2048:3072], 1024, 1024,
                          xcv, xcvk, ep_v, tag="l1fv")
        stream_project_fm(B, E, None, aw[:, 1024:2048], 1024, 1024,
                          xcv, xcvk, mkep(kfb, "kfb"), tag="l1fk")
        stream_project_fm(B, E, None, aw[:, 0:1024], 1024, 1024,
                          xcv, xcvk, mkep(qf, "qf"), tag="l1fq")
        B.barrier()
        s_cv.close()

        for (buf, bkey) in ((qf, "qf"), (kfb, "kfb")):
            for kc in range(KC):
                for tt in range(4):
                    sl = slice(tt * 512, (tt + 1) * 512)
                    psr, pkr = E.ps()
                    B.t(lambda t, o=psr[:, 0:512], r=rot64[:, :],
                        x=buf[:, kc, sl]:
                        t.matmul(o, r, x, start=True, stop=True),
                        reads=((bkey, kc, tt), ("rot64",)), writes=(pkr,),
                        tag="rope_mm")
                    B.v(lambda e, dd=buf[:, kc, sl], c=rope[:, 0, sl]:
                        e.tensor_mul(dd, dd, c),
                        reads=((bkey, kc, tt), ("rope",)),
                        writes=((bkey, kc, tt),), tag="rope_c")
                    B.v(lambda e, a=exps[:, 3, :], s=psr[:, 0:512],
                        c=rope[:, 1, sl]: e.tensor_mul(a, s, c),
                        reads=(pkr, ("rope",)), writes=(("ropes",),),
                        tag="rope_s")
                    B.v(lambda e, dd=buf[:, kc, sl], a=exps[:, 3, :]:
                        e.tensor_add(dd, dd, a),
                        reads=((bkey, kc, tt), ("ropes",)),
                        writes=((bkey, kc, tt),), tag="rope_a")

        for i, (buf, bkey) in enumerate(((qf, "qf"), (kfb, "kfb"))):
            B.v(lambda e, dd=cols[:, 40:41], s=buf[:, :, :]
                .rearrange("p a b -> p (a b)"):
                e.reduce_max(dd, s, axis=AX.X, apply_absolute_value=True),
                reads=tuple((bkey, kc, tt) for kc in range(KC)
                            for tt in range(4)),
                writes=(("qkm", i),), tag="qkmax")
            ps_t, pk_t = E.ps()
            B.t(lambda t, o=ps_t[0:1, 0:128], s=cols[:, 40:41],
                i2=idf[:, :]: t.transpose(o, s, i2),
                reads=(("qkm", i),), writes=(pk_t,), tag="qkmt")
            B.v(lambda e, dd=srow[0:1, i:i + 1], s=ps_t[0:1, 0:128]:
                e.reduce_max(dd, s, axis=AX.X),
                reads=(pk_t,), writes=(("qkg", i),), tag="qkg")
        B.v(lambda e, dd=srow[0:1, 2:3], a=srow[0:1, 0:1], b=srow[0:1, 1:2]:
            e.tensor_mul(dd, a, b),
            reads=(("qkg", 0), ("qkg", 1)), writes=(("qkb",),), tag="qkb")
        B.v(lambda e, dd=srow[0:1, 2:3]:
            e.tensor_scalar(dd, dd, -8.0, None, ALU.mult),
            reads=(("qkb",),), writes=(("qkb",),), tag="qkb2")
        B.v(lambda e, dd=scr["rows_bf"][0:1, 0:1], s=srow[0:1, 2:3]:
            e.tensor_copy(dd, s),
            reads=(("qkb",),), writes=(("qkbf",),), tag="qkbc")
        ps_b, pk_b = E.ps()
        B.t(lambda t, o=ps_b[:, 0:1], on=ones[0:1, :],
            r=scr["rows_bf"][0:1, 0:1]:
            t.matmul(o, on, r, start=True, stop=True),
            reads=(("qkbf",),), writes=(pk_b,), tag="qkbb")
        B.v(lambda e, dd=bcol[:, 0:1], s=ps_b[:, 0:1]: e.tensor_copy(dd, s),
            reads=(pk_b,), writes=(("bcol",),), tag="qkbe")
        B.barrier()

        E.psf_n = 4
        PVB = (E.psf[4], E.psf[5])
        PVK = (("psf", 4), ("psf", 5))
        for hg in range(0, NH, 2):
            for qt in range(4):
                qsl = slice(qt * 512, (qt + 1) * 512)
                njc = (qt + 1) * 4
                pend = []
                for jc in range(njc):
                    for lane in range(2):
                        h = hg + lane
                        kc = h // 2
                        rb = (h % 2) * 64
                        par = (jc * 2 + lane) % 4
                        pss, pks = E.ps()
                        diag = jc >= njc - 4
                        B.t(lambda t, o=pss[0:128, 0:512],
                            l=kfb[rb:rb + 64, kc, jc * 128:(jc + 1) * 128],
                            r=qf[rb:rb + 64, kc, qsl], sp=not diag:
                            t.matmul(o, l, r, start=True, stop=sp),
                            reads=(("kfb", kc, jc // 4), ("qf", kc, qt)),
                            writes=(pks,), tag=f"sc{h},{qt},{jc}")
                        if diag:
                            b = jc - (njc - 4)
                            B.t(lambda t, o=pss[0:128, 0:512], l=idn,
                                r=cmask[:, b, :]:
                                t.matmul(o, l, r, start=False, stop=True),
                                reads=(("cmask",),), writes=(pks,),
                                tag=f"msk{h},{qt},{jc}")
                        B.s(lambda e, dd=exps[:, par, :], s=pss[0:128, 0:512],
                            b2=bcol[:, 0:1]:
                            e.activation(dd, s, AF.Exp, bias=b2, scale=0.125),
                            reads=(pks, ("bcol",)), writes=(("exp", par),),
                            tag=f"exp{h},{qt},{jc}")

                        def em_pv(lane=lane, h=h, jc=jc, par=par):
                            B.t(lambda t, o=PVB[lane][0:65, 0:512],
                                l=vt[:, jc, h * 65:(h + 1) * 65],
                                r=exps[:, par, :],
                                st=(jc == 0), sp=(jc == njc - 1):
                                t.matmul(o, l, r, start=st, stop=sp),
                                reads=(("exp", par), ("vt", jc, h // 8),
                                       ("vt", jc, "ones")),
                                writes=(PVK[lane],), tag=f"pv{h},{qt},{jc}")
                        pend.append(em_pv)
                        if len(pend) > 2:
                            pend.pop(0)()
                for em in pend:
                    em()
                for lane in range(2):
                    h = hg + lane
                    kc = h // 2
                    rb = (h % 2) * 64
                    B.s(lambda e, dd=attr[64:65, lane, :],
                        s=PVB[lane][64:65, 0:512]: e.activation(dd, s, AF.Ln),
                        reads=(PVK[lane],), writes=(("rsum", lane),),
                        tag=f"rcp{h},{qt}")
                    B.s(lambda e, dd=scr["bc"][64:65, lane, 0:512],
                        s=attr[64:65, lane, :]:
                        e.activation(dd, s, AF.Exp, scale=-1.0),
                        reads=(("rsum", lane),), writes=(("rsumb", lane),),
                        tag=f"rcb{h},{qt}")
                    ps_r, pk_r = E.ps()
                    B.t(lambda t, o=ps_r[0:64, 0:512], on=ones[64:65, 0:64],
                        r=scr["bc"][64:65, lane, 0:512]:
                        t.matmul(o, on, r, start=True, stop=True),
                        reads=(("rsumb", lane),), writes=(pk_r,),
                        tag=f"rbc{h},{qt}")
                    B.v(lambda e, dd=rbc[:, lane, :], s=ps_r[0:64, 0:512]:
                        e.tensor_copy(dd, s),
                        reads=(pk_r,), writes=(("rbc", lane),),
                        tag=f"rbce{h},{qt}")
                    B.v(lambda e, dd=qf[rb:rb + 64, kc, qsl],
                        s=PVB[lane][0:64, 0:512], r=rbc[:, lane, :]:
                        e.tensor_mul(dd, s, r),
                        reads=(PVK[lane], ("rbc", lane)),
                        writes=(("qf", kc, qt),), tag=f"ao{h},{qt}")
        E.psf_n = 6
        B.barrier()
        s_kv.close()
        s_ropes.close()

        with ExitStack() as s_xs2:
            xs = E.sb("l1f_xs1", [128, KC, T], F32, s_xs2, side="right")
            load_fm_f32(B, E, xs, "xs", spill, tag="unspill")

            def ep_proj(oc, tt, ps, pk, ow):
                B.v(lambda e, dd=xs[:, oc, tt * 512:(tt + 1) * 512], s=ps:
                    e.tensor_add(dd, dd, s),
                    reads=(pk, ("xs", oc, tt)), writes=(("xs", oc, tt),),
                    tag="epproj")
            stream_project_fm(B, E, None, d["L1_ffn_proj_w"], 1024, 1024,
                              qf, lambda kc, tt: ("qf", kc, tt), ep_proj,
                              tag="l1fproj")
            B.barrier()
            s_qf.close()

            with ExitStack() as c_moe:
                xn2 = E.sb("l1f_xn2", [128, KC, T], BF16, c_moe)
                lgt = E.sb("l1f_lgt", [4, T], F32, c_moe)
                msk_tm = E.sb("l1f_msktm", [128, 4], BF16, c_moe)
                msk_fm = E.sb("l1f_mskfm", [4, T], BF16, c_moe)
                mkbc = E.sb("l1f_mkbc", [128, 4, 512], BF16, c_moe)
                gq = E.sb("l1f_gq", [128, 4, T], BF16, c_moe)
                vq = E.sb("l1f_vq", [128, 4, T], BF16, c_moe)
                xnk = lambda kc, tt: ("xn2", kc, tt)
                emit_ln(B, E, xs, xsk, xn2, "xn2", KC, T, kind="ln",
                        g_col=cols[:, 16:24], b_col=cols[:, 24:32],
                        g_key=("cols",), eps=1e-5, scr=scr, src_is_f32=True,
                        tag="l1f_ln2")
                with ExitStack() as c_r:
                    rw = E.sb("l1f_rw", [128, KC, 4], BF16, c_r)
                    B.dma("gpsimd", rw[:, :, :],
                          d["L1_ffn_router_w"].rearrange("(k p) n -> p k n",
                                                         p=128),
                          writes=(("rw",),))
                    for tt in range(4):
                        psl, pkl = E.ps()
                        parts = [(rw[:, kc2, 0:4],
                                  xn2[:, kc2, tt * 512:(tt + 1) * 512])
                                 for kc2 in range(KC)]
                        rk = [("rw",)] + [("xn2", kc2, tt) for kc2 in range(KC)]
                        mm_group(B, psl[0:4, 0:512], pkl, parts, rk,
                                 tag=f"rt{tt}")
                        B.v(lambda e, dd=lgt[0:4, tt * 512:(tt + 1) * 512],
                            s=psl[0:4, 0:512]: e.tensor_copy(dd, s),
                            reads=(pkl,), writes=(("lgt", tt),), tag=f"rte{tt}")
                    for tb in range(16):
                        tsl = slice(tb * 128, (tb + 1) * 128)
                        pst, pkt = E.ps()
                        B.t(lambda t, o=pst[0:128, 0:4], s=lgt[0:4, tsl],
                            i=idf[0:4, 0:4]: t.transpose(o, s, i),
                            reads=(("lgt", tb // 4),), writes=(pkt,),
                            tag=f"lt{tb}")
                        B.v(lambda e, dd=cols[:, 40:41], s=pst[0:128, 0:4]:
                            e.reduce_max(dd, s, axis=AX.X),
                            reads=(pkt,), writes=(("lmax",),), tag=f"lm{tb}")
                        B.v(lambda e, dd=msk_tm[:, 0:4], s=pst[0:128, 0:4],
                            m=cols[:, 40:41]:
                            e.tensor_scalar(dd, s, m, None, ALU.is_ge),
                            reads=(pkt, ("lmax",)), writes=(("msktm",),),
                            tag=f"lq{tb}")
                        psm, pkm = E.pst()
                        B.t(lambda t, o=psm[0:4, 0:128], s=msk_tm[:, 0:4],
                            i=idn: t.transpose(o, s, i),
                            reads=(("msktm",),), writes=(pkm,), tag=f"lmt{tb}")
                        B.v(lambda e, dd=msk_fm[0:4, tsl], s=psm[0:4, 0:128]:
                            e.tensor_copy(dd, s),
                            reads=(pkm,), writes=(("mskfm", tb),),
                            tag=f"lme{tb}")
                B.barrier()

                wv = d["L1_ffn_expert_wv"]
                wp = d["L1_ffn_expert_proj"]
                for e_i in range(4):
                    for tt in range(4):
                        tsl = slice(tt * 512, (tt + 1) * 512)
                        ps1, pk1 = E.ps()
                        B.t(lambda t, o=ps1[0:1, 0:512],
                            l=idn[0:4, e_i:e_i + 1], r=msk_fm[0:4, tsl]:
                            t.matmul(o, l, r, start=True, stop=True),
                            reads=tuple(("mskfm", tb) for tb in
                                        range(tt * 4, tt * 4 + 4)),
                            writes=(pk1,), tag=f"mr{e_i},{tt}")
                        B.v(lambda e, dd=scr["rows_bf"][0:1, 0:512],
                            s=ps1[0:1, 0:512]: e.tensor_copy(dd, s),
                            reads=(pk1,), writes=(("mrow",),),
                            tag=f"mre{e_i},{tt}")
                        ps2, pk2 = E.ps()
                        B.t(lambda t, o=ps2[:, 0:512], on=ones[0:1, :],
                            r=scr["rows_bf"][0:1, 0:512]:
                            t.matmul(o, on, r, start=True, stop=True),
                            reads=(("mrow",),), writes=(pk2,),
                            tag=f"mb{e_i},{tt}")
                        B.op(E.rr(), (lambda e, dd=mkbc[:, tt, :],
                                      s=ps2[:, 0:512]: ecopy(e, dd, s)),
                             reads=(pk2,), writes=(("mkbc", tt),),
                             tag=f"mbe{e_i},{tt}")
                    for sl8 in range(8):
                        def ep_g(oc, tt, ps, pk, ow):
                            B.s(lambda e,
                                dd=gq[:, oc, tt * 512:(tt + 1) * 512],
                                s=ps: e.activation(dd, s, AF.Silu),
                                reads=(pk,), writes=(("gq", oc, tt),),
                                tag="epg")

                        def ep_vv(oc, tt, ps, pk, ow):
                            B.v(lambda e,
                                dd=vq[:, oc, tt * 512:(tt + 1) * 512],
                                s=ps, m=mkbc[:, tt, :]:
                                e.tensor_mul(dd, s, m),
                                reads=(pk, ("mkbc", tt)),
                                writes=(("vq", oc, tt),), tag="epvv")
                        stream_project_fm(B, E, None,
                                          wv[e_i][:, sl8 * 512:(sl8 + 1) * 512],
                                          1024, 512, xn2, xnk, ep_g,
                                          tag=f"g{e_i}_{sl8}")
                        stream_project_fm(B, E, None,
                                          wv[e_i][:, 4096 + sl8 * 512:
                                                  4096 + (sl8 + 1) * 512],
                                          1024, 512, xn2, xnk, ep_vv,
                                          tag=f"v{e_i}_{sl8}")
                        for oc in range(4):
                            for tt in range(4):
                                tsl = slice(tt * 512, (tt + 1) * 512)
                                B.v(lambda e, dd=gq[:, oc, tsl],
                                    v2=vq[:, oc, tsl]:
                                    e.tensor_mul(dd, dd, v2),
                                    reads=(("gq", oc, tt), ("vq", oc, tt)),
                                    writes=(("gq", oc, tt),),
                                    tag=f"gv{e_i}{sl8}{oc}{tt}")
                        wtile = E.wtile(4, 1024)
                        B.dma("gpsimd", wtile[0][:, :, :],
                              wp[e_i][sl8 * 512:(sl8 + 1) * 512, :]
                              .rearrange("(k p) n -> p k n", p=128),
                              writes=(wtile[1],), tag=f"pj{e_i}{sl8}")
                        for oc in range(8):
                            for tt in range(4):
                                psj, pkj = E.ps()
                                parts = [(wtile[0][:, kc2,
                                                   oc * 128:(oc + 1) * 128],
                                          gq[:, kc2, tt * 512:(tt + 1) * 512])
                                         for kc2 in range(4)]
                                rk = [wtile[1]] + [("gq", kc2, tt)
                                                   for kc2 in range(4)]
                                mm_group(B, psj[:, 0:512], pkj, parts, rk,
                                         tag=f"pjm{e_i}{sl8}{oc}{tt}")
                                B.v(lambda e,
                                    dd=xs[:, oc, tt * 512:(tt + 1) * 512],
                                    s=psj[:, 0:512]: e.tensor_add(dd, dd, s),
                                    reads=(pkj, ("xs", oc, tt)),
                                    writes=(("xs", oc, tt),), tag="pja")
                B.barrier()
            with ExitStack() as c_f:
                fin2 = E.sb("l1f_fin", [128, 2, 512], F32, c_f)
                emit_residual_finale(B, E, xs, xsk, src_dram, dst_dram, fin2,
                                     tag="l1f_fin")
                B.barrier()



L0F_DEPTH = 6


def phase_l0f(B, E, d, src_dram, dst_dram):
    nc = E.nc
    KC, T, DEPTH = 8, 2048, L0F_DEPTH
    NH, DH, W = 16, 64, 256
    NW = T // W
    idn = E.consts["id"]
    idf = E.consts["idf"]
    ones = E.consts["ones"]
    spill = nc.dram_tensor("l0f_spill", [1024, 2048], F32, kind="Internal").ap()
    with ExitStack() as pctx:
        scr = {
            "xb": E.sb("l0f_sxb", [128, 2, 512], BF16, pctx),
            "sq": E.sb("l0f_ssq", [128, 2, 512], BF16, pctx),
            "rows": E.sb("l0f_srows", [1, 1536], F32, pctx),
            "rows_bf": E.sb("l0f_srbf", [1, 1024], BF16, pctx),
            "bc": E.sb("l0f_sbc", [128, 2, 512], BF16, pctx),
        }
        cols = E.sb("l0f_cols", [128, 104], F32, pctx)
        vstage = E.sb("l0f_vstage", [72, 128], F32, pctx)
        ropes = E.sb("l0f_rope", [128, 2, T], BF16, pctx)
        wmask = E.sb("l0f_wmask", [128, 4, 256], BF16, pctx)
        hmask = E.sb("l0f_hmask", [128, 2], BF16, pctx)
        hmaskT = E.sb("l0f_hmaskT", [2, 128], BF16, pctx)
        rot64 = E.sb("l0f_rot", [128, 128], BF16, pctx)
        rows2 = E.sb("l0f_rows2", [2, 4, 512], F32, pctx)
        rows2b = E.sb("l0f_rows2b", [2, 4, 512], BF16, pctx)
        attr = E.sb("l0f_attr", [128, 2, 256], F32, pctx)
        srow = E.sb("l0f_srow", [1, 16], F32, pctx)
        bcol = E.sb("l0f_bcol", [128, 1], F32, pctx)
        exps = E.sb("l0f_exps", [128, 4, 256], BF16, pctx)
        rbc = E.sb("l0f_rbc", [64, 2, 256], BF16, pctx)

        B.dma("sync", wmask[:, :, :], d["c_l0f_masks"].rearrange("a p t -> p a t"),
              writes=(("wmask",),))
        B.dma("sync", hmask[:, :], d["c_hmask"], writes=(("hmask",),))
        B.dma("sync", hmaskT[:, :], d["c_hmaskT"], writes=(("hmaskT",),))
        B.dma("sync", rot64[:, :], d["c_rot64"], writes=(("rot64",),))

        xsk = lambda kc, tt: ("xs", kc, tt)
        s_xs = ExitStack()
        xs = E.sb("l0f_xs", [128, KC, T], F32, s_xs, side="right")
        load_fm_f32(B, E, xs, "xs", src_dram)
        emit_ln(B, E, xs, xsk, xs, "xs", KC, T, kind="ln", eps=1e-5, scr=scr,
                src_is_f32=True, tag="l0f_ln0")
        B.barrier()

        for li in range(DEPTH):
            load_vec_cols(B, E, [d["L0_ffn_ln_g"][li], d["L0_ffn_ln_b"][li],
                                 d["L0_ffn_b2"][li], d["L0_ffn_q_scale"][li],
                                 d["L0_ffn_k_scale"][li], d["L0_ffn_b1re"][li]],
                          vstage, cols, ("cols",), tag=f"l0f_vc{li}")

            s_sh = ExitStack()
            s_q = ExitStack()
            s_kv = ExitStack()
            if True:
                xsh = E.sb(f"l0f_xsh{li}", [128, KC, T], BF16, s_sh)
                for kc in range(KC):
                    if kc < 4:
                        for tt in range(4):
                            sl = slice(tt * 512, (tt + 1) * 512)
                            B.op(E.rr(), (lambda e, dd=xsh[:, kc, sl],
                                          s=xs[:, kc, sl]: ecopy(e, dd, s)),
                                 reads=(("xs", kc, tt),),
                                 writes=(("xsh", kc, tt),), tag="shc")
                    else:
                        B.v(lambda e, dd=xsh[:, kc, 0:1]: e.memset(dd, 0.0),
                            writes=(("xsh", kc, 0),), tag="shz")
                        for tt in range(4):
                            lo = tt * 512
                            B.op(E.rr(), (lambda e,
                                          dd=xsh[:, kc, lo + (1 if tt == 0 else 0):
                                                 lo + 512],
                                          s=xs[:, kc, lo + (0 if tt == 0 else -1):
                                               lo + 511]: ecopy(e, dd, s)),
                                 reads=(("xs", kc, tt),
                                        ("xs", kc, max(tt - 1, 0))),
                                 writes=(("xsh", kc, tt),), tag="shs")
                for kc in range(KC):
                    B.dma("scalar", spill[kc * 128:(kc + 1) * 128, :],
                          xs[:, kc, :],
                          reads=tuple(("xs", kc, tt) for tt in range(4)),
                          tag=f"l0fspill{li}_{kc}")
                B.barrier()
                s_xs.close()

                xshk = lambda kc, tt: ("xsh", kc, tt)
                wq = d["L0_ffn_Wqkv"][li]

                def mkep(buf, bk):
                    def ep(oc, tt, ps, pk, ow):
                        B.op(E.rr(), (lambda e,
                                      dd=buf[:, oc, tt * 512:(tt + 1) * 512],
                                      s=ps: ecopy(e, dd, s)),
                             reads=(pk,), writes=((bk, oc, tt),), tag="ep" + bk)
                    return ep

                def ep_v(tc, fs, ps, pk, fw):
                    nh8 = fw // 64
                    B.op(E.rr(), (lambda e,
                                  dd=vt[:, tc, fs * 8 * 65:(fs * 8 + nh8) * 65]
                                  .rearrange("p (h c) -> p h c", c=65)
                                  [:, :, 0:64],
                                  s=ps.rearrange("p (h c) -> p h c", c=64):
                                  ecopy(e, dd, s)),
                         reads=(pk,), writes=(("vt", tc, fs),), tag="epv")
                    if fs == 0:
                        B.v(lambda e, dd=vt[:, tc, :]
                            .rearrange("p (h c) -> p h c", c=65)[:, :, 64:65]:
                            e.memset(dd, 1.0),
                            writes=(("vt", tc, "ones"),), tag="epvone")
                qf = E.sb(f"l0f_qf{li}", [128, KC, T], BF16, s_q)
                kf = E.sb(f"l0f_kf{li}", [128, KC, T], BF16, s_kv)
                vt = E.sb(f"l0f_vt{li}", [128, T // 128, 16 * 65], BF16, s_kv)
                stream_project_tm(B, E, None, wq[:, 2048:3072], 1024, 1024,
                                  xsh, xshk, ep_v, tag=f"l0fv{li}")
                stream_project_fm(B, E, None, wq[:, 1024:2048], 1024, 1024,
                                  xsh, xshk, mkep(kf, "kf"), tag=f"l0fk{li}")
                stream_project_fm(B, E, None, wq[:, 0:1024], 1024, 1024,
                                  xsh, xshk, mkep(qf, "qf"), tag=f"l0fq{li}")

            if li == 0:
                E.dump("owm", wmask[:, :, :].rearrange("p a b -> p (a b)"))
                E.dump("oqraw", qf[:, :, :].rearrange("p a b -> p (a b)"))
            B.v(lambda e, dd=cols[0:2, 70:71]: e.memset(dd, 1e-24),
                writes=(("l2eps",),), tag="l2epsc")
            for (buf, bkey, scol) in ((qf, "qf", 24), (kf, "kf", 25)):
                for kc in range(KC):
                    for tt in range(4):
                        sl = slice(tt * 512, (tt + 1) * 512)
                        rot = (kc * 4 + tt) % 4
                        B.s(lambda e, dd=scr["sq"][:, tt % 2, :],
                            s=buf[:, kc, sl]: e.activation(dd, s, AF.Square),
                            reads=((bkey, kc, tt),),
                            writes=(("l2sq", tt % 2),), tag="l2sq")
                        ps_s, pk_s = E.ps()
                        B.t(lambda t, o=ps_s[0:2, 0:512], l=hmask[:, 0:2],
                            r=scr["sq"][:, tt % 2, :]:
                            t.matmul(o, l, r, start=True, stop=True),
                            reads=(("l2sq", tt % 2), ("hmask",)),
                            writes=(pk_s,), tag="l2mm")
                        B.s(lambda e, dd=rows2[0:2, rot, :],
                            s=ps_s[0:2, 0:512], ba=cols[0:2, 70:71]:
                            e.activation(dd, s, AF.Ln, bias=ba, scale=1.0),
                            reads=(pk_s, ("l2eps",)), writes=(("l2r", rot),),
                            tag="l2ln")
                        B.s(lambda e, dd=rows2b[0:2, rot, :],
                            s=rows2[0:2, rot, :]:
                            e.activation(dd, s, AF.Exp, scale=-0.5),
                            reads=(("l2r", rot),), writes=(("l2rb", rot),),
                            tag="l2ex")
                        ps_b, pk_b = E.ps()
                        B.t(lambda t, o=ps_b[:, 0:512], l=hmaskT[0:2, :],
                            r=rows2b[0:2, rot, :]:
                            t.matmul(o, l, r, start=True, stop=True),
                            reads=(("l2rb", rot), ("hmaskT",)), writes=(pk_b,),
                            tag="l2bc")
                        B.v(lambda e, dd=buf[:, kc, sl], q=buf[:, kc, sl],
                            sc=cols[:, scol:scol + 1], bb=ps_b[:, 0:512]:
                            e.scalar_tensor_tensor(dd, q, sc, bb, ALU.mult,
                                                   ALU.mult),
                            reads=((bkey, kc, tt), ("cols",), pk_b),
                            writes=((bkey, kc, tt),), tag="l2ap")
            for gi, (buf, bkey, ci, si) in enumerate(
                    ((qf, "qf", 0, 0), (kf, "kf", 2, 0))):
                B.dma("sync", ropes[:, :, :],
                      d["c_l0f_rope"][ci:ci + 2, :, :]
                      .rearrange("a p t -> p a t"),
                      writes=(("rope",),), tag=f"ropeld{li}{gi}")
                ci, si = 0, 1
                rk_rope = ("rope",)
                for kc in range(KC):
                    for tt in range(4):
                        sl = slice(tt * 512, (tt + 1) * 512)
                        psr, pkr = E.ps()
                        B.t(lambda t, o=psr[:, 0:512], r=rot64[:, :],
                            x=buf[:, kc, sl]:
                            t.matmul(o, r, x, start=True, stop=True),
                            reads=((bkey, kc, tt), ("rot64",)), writes=(pkr,),
                            tag="xp_mm")
                        B.v(lambda e, dd=buf[:, kc, sl], c=ropes[:, ci, sl]:
                            e.tensor_mul(dd, dd, c),
                            reads=((bkey, kc, tt), rk_rope),
                            writes=((bkey, kc, tt),), tag="xp_c")
                        B.v(lambda e, a=scr["bc"][:, 0, :], s=psr[:, 0:512],
                            c=ropes[:, si, sl]: e.tensor_mul(a, s, c),
                            reads=(pkr, rk_rope), writes=(("xps",),),
                            tag="xp_s")
                        B.v(lambda e, dd=buf[:, kc, sl], a=scr["bc"][:, 0, :]:
                            e.tensor_add(dd, dd, a),
                            reads=((bkey, kc, tt), ("xps",)),
                            writes=((bkey, kc, tt),), tag="xp_a")
            if li == 0:
                E.dump("oq", qf[:, :, :].rearrange("p a b -> p (a b)"))
                E.dump("ok", kf[:, :, :].rearrange("p a b -> p (a b)"))
                E.dump("ov", vt[:, :, :].rearrange("p a b -> p (a b)"))
            for i, scol in enumerate((24, 25)):
                ps_t, pk_t = E.ps()
                B.t(lambda t, o=ps_t[0:1, 0:128], s=cols[:, scol:scol + 1],
                    i2=idf[:, :]: t.transpose(o, s, i2),
                    reads=(("cols",),), writes=(pk_t,), tag="sbt")
                B.v(lambda e, dd=srow[0:1, i:i + 1], s=ps_t[0:1, 0:128]:
                    e.reduce_max(dd, s, axis=AX.X, apply_absolute_value=True),
                    reads=(pk_t,), writes=(("sbm", i),), tag="sbm")
            B.v(lambda e, dd=srow[0:1, 2:3], a=srow[0:1, 0:1],
                b=srow[0:1, 1:2]: e.tensor_mul(dd, a, b),
                reads=(("sbm", 0), ("sbm", 1)), writes=(("sbb",),), tag="sbb")
            B.v(lambda e, dd=srow[0:1, 2:3]:
                e.tensor_scalar(dd, dd, -8.0, None, ALU.mult),
                reads=(("sbb",),), writes=(("sbb",),), tag="sbb2")
            B.v(lambda e, dd=scr["rows_bf"][0:1, 0:1], s=srow[0:1, 2:3]:
                e.tensor_copy(dd, s),
                reads=(("sbb",),), writes=(("sbbf",),), tag="sbc")
            ps_b2, pk_b2 = E.ps()
            B.t(lambda t, o=ps_b2[:, 0:1], on=ones[0:1, :],
                r=scr["rows_bf"][0:1, 0:1]:
                t.matmul(o, on, r, start=True, stop=True),
                reads=(("sbbf",),), writes=(pk_b2,), tag="sbbb")
            B.v(lambda e, dd=bcol[:, 0:1], s=ps_b2[:, 0:1]: e.tensor_copy(dd, s),
                reads=(pk_b2,), writes=(("bcol",),), tag="sbbe")

            E.psf_n = 4
            PVB = (E.psf[4], E.psf[5])
            PVK = (("psf", 4), ("psf", 5))
            for w in range(NW):
                wsl = slice(w * W, (w + 1) * W)
                jts = []
                if w > 0:
                    jts += [((w - 1) * W, 0), ((w - 1) * W + 128, 1)]
                jts += [(w * W, 2), (w * W + 128, 3)]
                nj = len(jts)
                for hg in range(0, NH, 2):
                    pend = []
                    for ji, (jabs, mi) in enumerate(jts):
                        for lane in range(2):
                            h = hg + lane
                            kc = h // 2
                            rb = (h % 2) * 64
                            par = (ji * 2 + lane) % 4
                            pss, pks = E.ps()
                            B.t(lambda t, o=pss[0:128, 0:256],
                                l=kf[rb:rb + 64, kc, jabs:jabs + 128],
                                r=qf[rb:rb + 64, kc, wsl]:
                                t.matmul(o, l, r, start=True, stop=False),
                                reads=(("kf", kc, jabs // 512),
                                       ("qf", kc, w // 2)),
                                writes=(pks,), tag=f"lsc{w},{h},{ji}")
                            B.t(lambda t, o=pss[0:128, 0:256], l=idn,
                                r=wmask[:, mi, :]:
                                t.matmul(o, l, r, start=False, stop=True),
                                reads=(("wmask",),), writes=(pks,),
                                tag=f"lscm{w},{h},{ji}")
                            B.s(lambda e, dd=exps[:, par, :],
                                s=pss[0:128, 0:256], b2=bcol[:, 0:1]:
                                e.activation(dd, s, AF.Exp, bias=b2,
                                             scale=8.0),
                                reads=(pks, ("bcol",)),
                                writes=(("lex", par),), tag=f"lxp{w},{h},{ji}")

                            def em_pv(lane=lane, h=h, ji=ji, jabs=jabs,
                                      par=par):
                                B.t(lambda t, o=PVB[lane][0:65, 0:256],
                                    l=vt[:, jabs // 128, h * 65:(h + 1) * 65],
                                    r=exps[:, par, :],
                                    st=(ji == 0), sp=(ji == nj - 1):
                                    t.matmul(o, l, r, start=st, stop=sp),
                                    reads=(("lex", par),
                                           ("vt", jabs // 128, h // 8),
                                           ("vt", jabs // 128, "ones")),
                                    writes=(PVK[lane],), tag=f"lpv{w},{h},{ji}")
                            pend.append(em_pv)
                            if len(pend) > 2:
                                pend.pop(0)()
                    for em in pend:
                        em()
                    for lane in range(2):
                        h = hg + lane
                        kc = h // 2
                        rb = (h % 2) * 64
                        rsl = slice(lane * 256, lane * 256 + 256)
                        B.s(lambda e, dd=attr[64:65, lane, :],
                            s=PVB[lane][64:65, 0:256]:
                            e.activation(dd, s, AF.Ln),
                            reads=(PVK[lane],), writes=(("lrs", lane),),
                            tag=f"lrc{w},{h}")
                        B.s(lambda e, dd=scr["bc"][64:65, lane, 0:256],
                            s=attr[64:65, lane, :]:
                            e.activation(dd, s, AF.Exp, scale=-1.0),
                            reads=(("lrs", lane),), writes=(("lrsb", lane),),
                            tag=f"lrb{w},{h}")
                        ps_r, pk_r = E.ps()
                        B.t(lambda t, o=ps_r[0:64, 0:256],
                            on=ones[64:65, 0:64],
                            r=scr["bc"][64:65, lane, 0:256]:
                            t.matmul(o, on, r, start=True, stop=True),
                            reads=(("lrsb", lane),), writes=(pk_r,),
                            tag=f"lbc{w},{h}")
                        B.v(lambda e, dd=rbc[:, lane, :], s=ps_r[0:64, 0:256]:
                            e.tensor_copy(dd, s),
                            reads=(pk_r,), writes=(("lrbc", lane),),
                            tag=f"lbe{w},{h}")
                        B.v(lambda e, dd=qf[rb:rb + 64, kc, wsl],
                            s=PVB[lane][0:64, 0:256], r=rbc[:, lane, :]:
                            e.tensor_mul(dd, s, r),
                            reads=(PVK[lane], ("lrbc", lane)),
                            writes=(("qf", kc, w // 2),),
                            tag=f"lao{w},{h}")
            E.psf_n = 6
            B.barrier()
            if li == 0:
                E.dump("oat", qf[:, :, :].rearrange("p a b -> p (a b)"))
            s_kv.close()

            xs = E.sb(f"l0f_xs{li}", [128, KC, T], F32, s_xs, side="right")
            load_fm_f32(B, E, xs, "xs", spill, tag=f"l0fus{li}")

            def ep_wo(oc, tt, ps, pk, ow):
                B.v(lambda e, dd=xs[:, oc, tt * 512:(tt + 1) * 512], s=ps:
                    e.tensor_add(dd, dd, s),
                    reads=(pk, ("xs", oc, tt)), writes=(("xs", oc, tt),),
                    tag="epwo")
            stream_project_fm(B, E, None, d["L0_ffn_Wo"][li], 1024, 1024,
                              qf, lambda kc, tt: ("qf", kc, tt), ep_wo,
                              tag=f"l0fwo{li}")
            B.barrier()
            if li == 0:
                E.dump("oxwo", xs[:, :, :].rearrange("p a b -> p (a b)"))
            s_q.close()
            s_sh.close()

            with ExitStack() as s_ffn:
                xsh2 = E.sb(f"l0f_xsh2_{li}", [128, KC, T], BF16, s_ffn)
                ha = E.sb(f"l0f_ha{li}", [128, 4, T], BF16, s_ffn)
                hg = E.sb(f"l0f_hg{li}", [128, 4, T], BF16, s_ffn)
                for kc in range(KC):
                    if kc < 4:
                        for tt in range(4):
                            sl = slice(tt * 512, (tt + 1) * 512)
                            B.op(E.rr(), (lambda e, dd=xsh2[:, kc, sl],
                                          s=xs[:, kc, sl]: ecopy(e, dd, s)),
                                 reads=(("xs", kc, tt),),
                                 writes=(("xsh2", kc, tt),), tag="sh2c")
                    else:
                        B.v(lambda e, dd=xsh2[:, kc, 0:1]: e.memset(dd, 0.0),
                            writes=(("xsh2", kc, 0),), tag="sh2z")
                        for tt in range(4):
                            lo = tt * 512
                            B.op(E.rr(), (lambda e,
                                          dd=xsh2[:, kc,
                                                  lo + (1 if tt == 0 else 0):
                                                  lo + 512],
                                          s=xs[:, kc,
                                               lo + (0 if tt == 0 else -1):
                                               lo + 511]: ecopy(e, dd, s)),
                                 reads=(("xs", kc, tt),
                                        ("xs", kc, max(tt - 1, 0))),
                                 writes=(("xsh2", kc, tt),), tag="sh2s")
                emit_ln(B, E, xsh2, lambda kc, tt: ("xsh2", kc, tt), xsh2,
                        "xsh2", KC, T, kind="ln", g_col=cols[:, 0:8],
                        b_col=cols[:, 8:16], g_key=("cols",), eps=1e-5,
                        scr=scr, src_is_f32=False, tag=f"l0f_fln{li}")
                xshk2 = lambda kc, tt: ("xsh2", kc, tt)
                slabs = [(0, 4), (4, 4), (8, 4), (12, 4), (16, 4), (20, 2)]
                for s_i, (c0, nch) in enumerate(slabs):
                    wid = nch * 128

                    def ep_a(oc, tt, ps, pk, ow, c0=c0):
                        B.v(lambda e, dd=ha[:, oc, tt * 512:(tt + 1) * 512],
                            s=ps, bb=cols[:, 26 + c0 + oc:27 + c0 + oc]:
                            e.tensor_scalar(dd, s, bb, None, ALU.add),
                            reads=(pk, ("cols",)), writes=(("ha", oc, tt),),
                            tag="epa")

                    def ep_gl(oc, tt, ps, pk, ow, c0=c0):
                        B.s(lambda e, dd=hg[:, oc, tt * 512:(tt + 1) * 512],
                            s=ps, bb=cols[:, 48 + c0 + oc:49 + c0 + oc]:
                            e.activation(dd, s, AF.Gelu, bias=bb, scale=1.0),
                            reads=(pk, ("cols",)), writes=(("hg", oc, tt),),
                            tag="epgl")
                    w1 = d["L0_ffn_W1re"][li]
                    stream_project_fm(B, E, None,
                                      w1[:, c0 * 128:c0 * 128 + wid],
                                      1024, wid, xsh2, xshk2, ep_a,
                                      tag=f"l0fa{li}_{s_i}")
                    stream_project_fm(B, E, None,
                                      w1[:, 2816 + c0 * 128:2816 + c0 * 128 + wid],
                                      1024, wid, xsh2, xshk2, ep_gl,
                                      tag=f"l0fg{li}_{s_i}")
                    for oc in range(nch):
                        for tt in range(4):
                            sl = slice(tt * 512, (tt + 1) * 512)
                            B.v(lambda e, dd=ha[:, oc, sl], g=hg[:, oc, sl]:
                                e.tensor_mul(dd, dd, g),
                                reads=(("ha", oc, tt), ("hg", oc, tt)),
                                writes=(("ha", oc, tt),), tag="hmul")
                    w2 = d["L0_ffn_W2re"][li]
                    for os_ in range(2):
                        wtile = E.wtile(nch, 512)
                        B.dma("gpsimd", wtile[0][:, :, :],
                              w2[c0 * 128:c0 * 128 + wid,
                                 os_ * 512:(os_ + 1) * 512]
                              .rearrange("(k p) n -> p k n", p=128),
                              writes=(wtile[1],), tag=f"w2d{li}{s_i}{os_}")
                        for oci in range(4):
                            oc = os_ * 4 + oci
                            for tt in range(4):
                                psj, pkj = E.ps()
                                parts = [(wtile[0][:, kc2,
                                                   oci * 128:(oci + 1) * 128],
                                          ha[:, kc2, tt * 512:(tt + 1) * 512])
                                         for kc2 in range(nch)]
                                rk = [wtile[1]] + [("ha", kc2, tt)
                                                   for kc2 in range(nch)]
                                mm_group(B, psj[:, 0:512], pkj, parts, rk,
                                         tag=f"w2m{li}{s_i}{oc}{tt}")
                                if s_i == len(slabs) - 1:
                                    B.v(lambda e,
                                        dd=xs[:, oc, tt * 512:(tt + 1) * 512],
                                        s=psj[:, 0:512],
                                        bb=cols[:, 16 + oc:17 + oc]:
                                        e.scalar_tensor_tensor(
                                            dd, s, bb, dd, ALU.add, ALU.add),
                                        reads=(pkj, ("cols",), ("xs", oc, tt)),
                                        writes=(("xs", oc, tt),), tag="w2ab")
                                else:
                                    B.v(lambda e,
                                        dd=xs[:, oc, tt * 512:(tt + 1) * 512],
                                        s=psj[:, 0:512]: e.tensor_add(dd, dd, s),
                                        reads=(pkj, ("xs", oc, tt)),
                                        writes=(("xs", oc, tt),), tag="w2a")
                B.barrier()
        with ExitStack() as c_f:
            fin2 = E.sb("l0f_fin", [128, 2, 512], F32, c_f)
            emit_residual_finale(B, E, xs, xsk, src_dram, dst_dram, fin2,
                                 tag="l0f_fin")
            B.barrier()
        s_xs.close()



L0A_STAGE = 5
L0A_NHEADS = 8
L0A_NCH = 16
L0A_HOPS = 9999
L0A_SERIAL = 0


def phase_l0a(B, E, d, src_dram, dst_dram):
    STAGE = L0A_STAGE
    nc = E.nc
    KC, T = 8, 2048
    HV, DK, DV = 8, 128, 128
    NCH = 16
    idn = E.consts["id"]
    idf = E.consts["idf"]
    ones = E.consts["ones"]
    spill = nc.dram_tensor("l0a_spill", [1024, 2048], F32, kind="Internal").ap()
    z_dram = nc.dram_tensor("l0a_z", [1024, 2048], BF16, kind="Internal").ap()
    core_dram = nc.dram_tensor("l0a_core", [1024, 2048], BF16,
                               kind="Internal").ap()
    with ExitStack() as pctx:
        scr = {
            "xb": E.sb("l0a_sxb", [128, 2, 512], BF16, pctx),
            "sq": E.sb("l0a_ssq", [128, 2, 512], BF16, pctx),
            "rows": E.sb("l0a_srows", [1, 1536], F32, pctx),
            "rows_bf": E.sb("l0a_srbf", [1, 1024], BF16, pctx),
            "bc": E.sb("l0a_sbc", [128, 2, 512], BF16, pctx),
        }
        cols = E.sb("l0a_cols", [128, 24], F32, pctx)
        vstage = E.sb("l0a_vstage", [24, 128], F32, pctx)
        dn = E.sb("l0a_dn", [128, 5, 128], BF16, pctx)
        s_ba = ExitStack()
        ba8 = E.sb("l0a_ba8", [8, 2, T], F32, s_ba)
        ct24 = E.sb("l0a_ct24", [128, 24], F32, s_ba)
        ac24 = E.sb("l0a_ac24", [128, 24], F32, s_ba)
        acbc = E.sb("l0a_acbc", [128, 8], F32, s_ba)
        nbcol = E.sb("l0a_nbcol", [128, 2], F32, s_ba)
        cb24 = E.sb("l0a_cb24", [8, 3, 128], F32, s_ba)
        cbb = E.sb("l0a_cbb", [8, 3, 128], BF16, s_ba)
        ones8 = E.sb("l0a_ones8", [8, 128], F32, s_ba)
        onecol = E.sb("l0a_onecol", [8, 1], F32, s_ba)

        B.dma("sync", dn[:, :, :], d["c_dn"].rearrange("a p t -> p a t"),
              writes=(("dn",),))
        load_vec_cols(B, E, [d["L0_attn_w_ln1"], d["L0_attn_w_ln2"],
                             d["L0_attn_gate_norm_w"], d["L0_attn_dt_bias"],
                             d["L0_attn_A_log"]],
                      vstage, cols, ("cols",), tag="l0a_vc")
        B.v(lambda e, dd=cols[:, 19:21], s=cols[:, 0:8]:
            e.tensor_scalar(dd, s, 1.0, None, ALU.add) if False else None,
            reads=(), writes=(), tag="x") if False else None
        B.v(lambda e, dd=cols[:, 0:16]:
            e.tensor_scalar(dd, dd, 1.0, None, ALU.add),
            reads=(("cols",),), writes=(("cols",),), tag="wp1")
        B.v(lambda e, a=ones8[:, :]: e.memset(a, 1.0), writes=(("ones8",),),
            tag="o8")
        B.v(lambda e, a=onecol[:, :]: e.memset(a, 1.0), writes=(("onec",),),
            tag="oc")
        B.s(lambda e, dd=cols[:, 21:22], s=cols[:, 18:19]:
            e.activation(dd, s, AF.Exp),
            reads=(("cols",),), writes=(("gco",),), tag="gc1")
        B.v(lambda e, dd=cols[:, 21:22]:
            e.tensor_scalar(dd, dd, -1.0, None, ALU.mult),
            reads=(("gco",), ("cols",)), writes=(("gco",),), tag="gc2")

        xsk = lambda kc, tt: ("xs", kc, tt)
        s_C = ExitStack()
        s_A = ExitStack()
        s_B = ExitStack()
        xn = E.sb("l0a_xn", [128, KC, T], BF16, s_A, side="right")
        xs = E.sb("l0a_xs", [128, KC, T], F32, s_B, side="right")

        load_fm_f32(B, E, xs, "xs", src_dram)
        emit_ln(B, E, xs, xsk, xs, "xs", KC, T, kind="ln", eps=1e-5, scr=scr,
                src_is_f32=True, tag="l0a_ln0")
        emit_ln(B, E, xs, xsk, xn, "xn", KC, T, kind="rms",
                g_col=cols[:, 0:8], g_key=("cols",), eps=1e-6, scr=scr,
                src_is_f32=True, tag="l0a_rms1")
        for kc in range(KC):
            B.dma("scalar", spill[kc * 128:(kc + 1) * 128, :], xs[:, kc, :],
                  reads=tuple(("xs", kc, tt) for tt in range(4)),
                  tag=f"l0aspill{kc}")
        B.barrier()
        s_B.close()
        qkvp = E.sb("l0a_qkvp", [128, 16, T + 3], BF16, s_C)

        xnk = lambda kc, tt: ("xn", kc, tt)
        if STAGE < 1:
            s_A.close()
            s_C.close()
            s_ba.close()
            xs = E.sb("l0a_xs2", [128, KC, T], F32, s_B, side="right")
            load_fm_f32(B, E, xs, "xs", spill, tag="l0aus")
            B.barrier()
            with ExitStack() as c_f:
                fin2 = E.sb("l0a_fin", [128, 2, 512], F32, c_f)
                emit_residual_finale(B, E, xs, xsk, src_dram, dst_dram, fin2,
                                     tag="l0a_fin")
                B.barrier()
            s_B.close()
            return
        def ep_qkv(oc, tt, ps, pk, ow):
            B.op(E.rr(), (lambda e,
                          dd=qkvp[:, oc, 3 + tt * 512:3 + (tt + 1) * 512],
                          s=ps: ecopy(e, dd, s)),
                 reads=(pk,), writes=(("qkvp", oc, tt),), tag="epqkv")
        stream_project_fm(B, E, None, d["L0_attn_W_qkvz"][:, 0:2048],
                          1024, 2048, xn, xnk, ep_qkv, tag="l0aqkv")
        with ExitStack() as s_z:
            zstage = E.sb("l0a_zst", [128, 2, 512], BF16, s_z, side="right")

            def ep_z(oc, tt, ps, pk, ow):
                par = (oc * 4 + tt) % 2
                B.op(E.rr(), (lambda e, dd=zstage[:, par, :], s=ps:
                              ecopy(e, dd, s)),
                     reads=(pk,), writes=(("zst", par),), tag="epz")
                B.dma("scalar", z_dram[oc * 128:(oc + 1) * 128,
                                       tt * 512:(tt + 1) * 512],
                      zstage[:, par, :],
                      reads=(("zst", par),), tag=f"zd{oc},{tt}")
            stream_project_fm(B, E, None, d["L0_attn_W_qkvz"][:, 2048:3072],
                              1024, 1024, xn, xnk, ep_z, tag="l0az")
            with ExitStack() as s_w:
                wba = E.sb("l0a_wba", [128, KC, 16], BF16, s_w, side="right")
                B.dma("gpsimd", wba[:, :, :],
                      d["L0_attn_W_ba"].rearrange("(k p) n -> p k n", p=128),
                      writes=(("wba",),))
                for half, r0 in ((0, 0), (1, 8)):
                    for tt in range(4):
                        psb, pkb = E.ps()
                        parts = [(wba[:, kc2, r0:r0 + 8],
                                  xn[:, kc2, tt * 512:(tt + 1) * 512])
                                 for kc2 in range(KC)]
                        rk = [("wba",)] + [("xn", kc2, tt) for kc2 in range(KC)]
                        mm_group(B, psb[0:8, 0:512], pkb, parts, rk,
                                 tag=f"ba{half}{tt}")
                        if half == 0:
                            B.s(lambda e, dd=ba8[0:8, 0, tt * 512:(tt + 1) * 512],
                                s=psb[0:8, 0:512]:
                                e.activation(dd, s, AF.Sigmoid),
                                reads=(pkb,), writes=(("beta", tt),), tag="bsig")
                        else:
                            B.v(lambda e, dd=ba8[0:8, 1, tt * 512:(tt + 1) * 512],
                                s=psb[0:8, 0:512]: e.tensor_copy(dd, s),
                                reads=(pkb,), writes=(("arow", tt),), tag="bacp")
                with ExitStack() as s_g:
                    gtmp = E.sb("l0a_gtmp", [8, T], F32, s_g, side="right")
                    rka = tuple(("arow", tt) for tt in range(4))
                    B.s(lambda e, dd=gtmp[:, :], s=ba8[0:8, 1, :],
                        bb=cols[0:8, 17:18]:
                        e.activation(dd, s, AF.Abs, bias=bb),
                        reads=rka + (("cols",),), writes=(("gt",),), tag="g1")
                    B.s(lambda e, dd=gtmp[:, :]:
                        e.activation(dd, dd, AF.Exp, scale=-1.0),
                        reads=(("gt",),), writes=(("gt",),), tag="g2")
                    B.s(lambda e, dd=gtmp[:, :], bb=onecol[:, 0:1]:
                        e.activation(dd, dd, AF.Ln, bias=bb),
                        reads=(("gt",), ("onec",)), writes=(("gt",),), tag="g3")
                    B.s(lambda e, dd=ba8[0:8, 1, :], s=ba8[0:8, 1, :],
                        bb=cols[0:8, 17:18]:
                        e.activation(dd, s, AF.Relu, bias=bb),
                        reads=rka + (("cols",),), writes=(("grow",),), tag="g4")
                    B.v(lambda e, dd=ba8[0:8, 1, :], a=gtmp[:, :]:
                        e.tensor_add(dd, dd, a),
                        reads=(("grow",), ("gt",)), writes=(("grow",),), tag="g5")
                    B.v(lambda e, dd=ba8[0:8, 1, :], sc=cols[0:8, 21:22]:
                        e.tensor_scalar(dd, dd, sc, None, ALU.mult),
                        reads=(("grow",), ("gco",)), writes=(("grow",),),
                        tag="g6")
            B.barrier()
        s_A.close()
        if STAGE < 2:
            s_C.close()
            s_ba.close()
            xs = E.sb("l0a_xs2", [128, KC, T], F32, s_B, side="right")
            load_fm_f32(B, E, xs, "xs", spill, tag="l0aus")
            B.barrier()
            with ExitStack() as c_f:
                fin2 = E.sb("l0a_fin", [128, 2, 512], F32, c_f)
                emit_residual_finale(B, E, xs, xsk, src_dram, dst_dram, fin2,
                                     tag="l0a_fin")
                B.barrier()
            s_B.close()
            return

        with ExitStack() as s_cw:
            convw = E.sb("l0a_convw", [128, 16, 4], F32, s_cw, side="right")
            accf = E.sb("l0a_accf", [128, 512], F32, s_cw, side="right")
            B.dma("sync", convw[:, :, :],
                  d["L0_attn_conv_w"].rearrange("(k p) j -> p k j", p=128),
                  writes=(("convw",),))
            B.v(lambda e, a=qkvp[:, :, 0:3]: e.memset(a, 0.0),
                writes=(("qkvz0",),), tag="qz")
            for kc in range(16):
                for tt in (3, 2, 1, 0):
                    t0 = tt * 512
                    rk = (("qkvp", kc, tt), ("qkvz0",), ("convw",)) + \
                        ((("qkvp", kc, tt - 1),) if tt > 0 else ())
                    B.v(lambda e, a=accf[:, :], x=qkvp[:, kc, t0:t0 + 512],
                        w=convw[:, kc, 0:1]:
                        e.tensor_scalar(a, x, w, None, ALU.mult),
                        reads=rk, writes=(("cac",),), tag=f"c0{kc},{tt}")
                    for j in (1, 2):
                        B.v(lambda e, a=accf[:, :],
                            x=qkvp[:, kc, t0 + j:t0 + 512 + j],
                            w=convw[:, kc, j:j + 1]:
                            e.scalar_tensor_tensor(a, x, w, a, ALU.mult,
                                                   ALU.add),
                            reads=rk + (("cac",),), writes=(("cac",),),
                            tag=f"c{j}{kc},{tt}")
                    B.v(lambda e, a=accf[:, :],
                        x=qkvp[:, kc, t0 + 3:t0 + 515],
                        w=convw[:, kc, 3:4]:
                        e.scalar_tensor_tensor(a, x, w, a, ALU.mult, ALU.add),
                        reads=rk + (("cac",),), writes=(("cac",),),
                        tag=f"c3{kc},{tt}")
                    wkeys = (("qc", kc, tt),) + \
                        ((("qkvp", kc, tt + 1),) if tt < 3 else ())
                    B.s(lambda e, dd=qkvp[:, kc, t0 + 3:t0 + 515], s=accf[:, :]:
                        e.activation(dd, s, AF.Silu),
                        reads=(("cac",),), writes=wkeys, tag="csl")
            B.barrier()
        qkv = qkvp[:, :, 3:T + 3]
        qck = lambda kc: tuple(("qc", kc, tt) for tt in range(4))

        for kc in range(8):
            isq = kc < 4
            for tt in range(4):
                sl = slice(tt * 512, (tt + 1) * 512)
                B.s(lambda e, dd=scr["sq"][:, tt % 2, :], s=qkv[:, kc, sl]:
                    e.activation(dd, s, AF.Square),
                    reads=(("qc", kc, tt),), writes=(("nsq", tt % 2),),
                    tag="nsq")
                ps_s, pk_s = E.ps()
                B.t(lambda t, o=ps_s[0:1, 0:512], on=ones[:, 0:1],
                    x=scr["sq"][:, tt % 2, :]:
                    t.matmul(o, on, x, start=True, stop=True),
                    reads=(("nsq", tt % 2),), writes=(pk_s,), tag="nmm")
                B.s(lambda e, dd=scr["rows"][0:1, 0:512], s=ps_s[0:1, 0:512],
                    ba=E.scal(1e-6):
                    e.activation(dd, s, AF.Ln, bias=ba, scale=1.0),
                    reads=(pk_s,), writes=(("nr",),), tag="nln")
                if isq:
                    B.s(lambda e, dd=scr["rows_bf"][0:1, 0:512],
                        s=scr["rows"][0:1, 0:512],
                        ba=E.scal(-2.4260151319598084):
                        e.activation(dd, s, AF.Exp, bias=ba, scale=-0.5),
                        reads=(("nr",),), writes=(("nrb",),), tag="ncb")
                else:
                    B.s(lambda e, dd=scr["rows_bf"][0:1, 0:512],
                        s=scr["rows"][0:1, 0:512]:
                        e.activation(dd, s, AF.Exp, scale=-0.5),
                        reads=(("nr",),), writes=(("nrb",),), tag="ncb")
                ps_b, pk_b = E.ps()
                B.t(lambda t, o=ps_b[:, 0:512], on=ones[0:1, :],
                    r=scr["rows_bf"][0:1, 0:512]:
                    t.matmul(o, on, r, start=True, stop=True),
                    reads=(("nrb",),), writes=(pk_b,), tag="nbc")
                B.v(lambda e, dd=qkv[:, kc, sl], bb=ps_b[:, 0:512]:
                    e.tensor_mul(dd, dd, bb),
                    reads=(("qc", kc, tt), pk_b), writes=(("qc", kc, tt),),
                    tag="nap")
        B.barrier()
        E.dump("oqkv", qkvp[:, :, :].rearrange("p a b -> p (a b)"))
        E.dump("ogb", ba8[0:8, :, :].rearrange("p a b -> p (a b)"))
        if STAGE < 3:
            s_C.close()
            s_ba.close()
            xs = E.sb("l0a_xs2", [128, KC, T], F32, s_B, side="right")
            load_fm_f32(B, E, xs, "xs", spill, tag="l0aus")
            B.barrier()
            with ExitStack() as c_f:
                fin2 = E.sb("l0a_fin", [128, 2, 512], F32, c_f)
                emit_residual_finale(B, E, xs, xsk, src_dram, dst_dram, fin2,
                                     tag="l0a_fin")
                B.barrier()
            s_B.close()
            return

        with ExitStack() as s_scan:
            Sf = E.sb("l0a_Sf", [128, 2, HV * DV], F32, s_scan)
            Sb = E.sb("l0a_Sb", [128, 2, HV * DV], BF16, s_scan)
            wt_ = {}
            for nm in ("kk", "qk", "eui", "eus", "b0", "a0", "ub", "aq",
                       "qA", "vtm", "ktm", "rn", "At", "zs", "co"):
                wt_[nm] = E.sb("l0a_w" + nm, [128, 2, 128], BF16, s_scan)
            tb = [E.sb(f"l0a_tb{i}", [128, 2, 128], BF16, s_scan)
                  for i in range(2)]
            ab = [E.sb(f"l0a_ab{i}", [128, 2, 128], BF16, s_scan)
                  for i in range(2)]
            bb_ = [E.sb(f"l0a_bb{i}", [128, 2, 128], BF16, s_scan)
                   for i in range(2)]
            argx = E.sb("l0a_argx", [128, 2, 128], F32, s_scan)
            ztile = E.sb("l0a_zt", [128, 2, 128], BF16, s_scan)
            B.v(lambda e, a=Sf[:, :, :]: e.memset(a, 0.0), writes=(("Sf", 1),),
                tag="s0f")
            B.v(lambda e, a=Sb[:, :, :]: e.memset(a, 0.0), writes=(("Sb", 1),),
                tag="s0b")
            B.barrier()

            for ci in range(min(NCH, L0A_NCH)):
                c0 = ci * 128
                csl = slice(c0, c0 + 128)
                par = ci % 2
                B.v(lambda e, dd=cb24[:, 0, :], a=ones8[:, :],
                    b=ba8[0:8, 1, csl]:
                    e.tensor_tensor_scan(dd, a, b, 0.0, ALU.mult, ALU.add),
                    reads=(("grow",), ("ones8",)), writes=(("cb", ci, 0),),
                    tag=f"cs{ci}")
                B.v(lambda e, dd=cb24[:, 1, :], s=ba8[0:8, 0, csl]:
                    e.tensor_copy(dd, s),
                    reads=tuple(("beta", tt) for tt in range(4)),
                    writes=(("cb", ci, 1),), tag=f"cpb{ci}")
                B.v(lambda e, dd=cb24[:, 2, :], s=cb24[:, 0, :],
                    sc=cb24[:, 0, 127:128]:
                    e.tensor_scalar(dd, s, sc, None, ALU.subtract),
                    reads=(("cb", ci, 0),), writes=(("cb", ci, 2),),
                    tag=f"crv{ci}")
                B.v(lambda e, dd=cb24[:, 2, :]:
                    e.tensor_scalar(dd, dd, -1.0, None, ALU.mult),
                    reads=(("cb", ci, 2),), writes=(("cb", ci, 2),),
                    tag=f"crn{ci}")
                B.v(lambda e, dd=cbb[:, :, :], s=cb24[:, :, :]:
                    e.tensor_copy(dd, s),
                    reads=(("cb", ci, 0), ("cb", ci, 1), ("cb", ci, 2)),
                    writes=(("cbb", ci),), tag=f"cbc{ci}")
                ps_t, pk_t = E.ps()
                for grp in range(3):
                    B.t(lambda t, o=ps_t[:, grp * 8:grp * 8 + 8],
                        s=cb24[:, grp, :], i2=idf[0:8, 0:8]:
                        t.transpose(o, s, i2),
                        reads=(("cb", ci, 0), ("cb", ci, 1), ("cb", ci, 2)),
                        writes=(pk_t,), tag=f"ctt{ci}{grp}")
                B.v(lambda e, dd=ct24[:, 0:24], s=ps_t[:, 0:24]:
                    e.tensor_copy(dd, s),
                    reads=(pk_t,), writes=(("ct24", ci),), tag=f"cte{ci}")
                B.s(lambda e, dd=ac24[:, 0:24], s=ct24[:, 0:24]:
                    e.activation(dd, s, AF.Exp),
                    reads=(("ct24", ci),), writes=(("ac24", ci),), tag=f"ace{ci}")
                B.v(lambda e, dd=acbc[:, 0:8], a=ac24[:, 0:8], b=ac24[:, 16:24]:
                    e.tensor_mul(dd, a, b),
                    reads=(("ac24", ci),), writes=(("acbc", ci),), tag=f"acb{ci}")

                def head_body(h):
                    hp = h % 2
                    qc_ = qkv[:, h // 2, csl]
                    kc_ = qkv[:, 4 + h // 2, csl]
                    vc_ = qkv[:, 8 + h, csl]
                    kq = ("qc", h // 2)
                    kk_ = ("qc", 4 + h // 2)
                    kv_ = ("qc", 8 + h)
                    rq = qck(h // 2)
                    rk_ = qck(4 + h // 2)
                    rv = qck(8 + h)
                    S0b = Sb[:, (ci + 1) % 2, h * DV:(h + 1) * DV]
                    S0f = Sf[:, (ci + 1) % 2, h * DV:(h + 1) * DV]
                    kS0b = ("Sb", (ci + 1) % 2)
                    B.v(lambda e, dd=nbcol[:, hp:hp + 1],
                        s=ct24[:, 8 + h:9 + h]:
                        e.tensor_scalar(dd, s, -1.0, None, ALU.mult),
                        reads=(("ct24", ci),), writes=(("nb", hp),),
                        tag=f"nb{ci},{h}")
                    ps1, pk1 = E.ps()
                    B.t(lambda t, o=ps1[:, 0:128], l=kc_, r=kc_:
                        t.matmul(o, l, r, start=True, stop=True),
                        reads=rk_, writes=(pk1,), tag=f"kk{ci},{h}")
                    B.v(lambda e, dd=wt_["kk"][:, hp, :], s=ps1[:, 0:128]:
                        e.tensor_copy(dd, s),
                        reads=(pk1,), writes=(("kk", hp),), tag=f"kke{ci},{h}")
                    ps2, pk2 = E.ps()
                    B.t(lambda t, o=ps2[:, 0:128], l=kc_, r=qc_:
                        t.matmul(o, l, r, start=True, stop=True),
                        reads=rk_ + rq, writes=(pk2,), tag=f"qk{ci},{h}")
                    B.v(lambda e, dd=wt_["qk"][:, hp, :], s=ps2[:, 0:128]:
                        e.tensor_copy(dd, s),
                        reads=(pk2,), writes=(("qk", hp),), tag=f"qke{ci},{h}")
                    if L0A_HOPS < 2:
                        return
                    if L0A_HOPS < 18:
                        return
                    ps3, pk3 = E.ps()
                    B.t(lambda t, o=ps3[0:1, 0:128], l=idf[0:8, h:h + 1],
                        r=cb24[:, 0, :]: t.matmul(o, l, r, start=True, stop=True),
                        reads=(("cb", ci, 0),), writes=(pk3,), tag=f"cr{ci},{h}")
                    B.v(lambda e, dd=scr["rows"][0:1, 512 + hp * 128:
                                                  640 + hp * 128],
                        s=ps3[0:1, 0:128]: e.tensor_copy(dd, s),
                        reads=(pk3,), writes=(("crow", hp),),
                        tag=f"cre{ci},{h}")
                    if L0A_HOPS < 19:
                        return
                    ps4, pk4 = E.ps()
                    B.t(lambda t, o=ps4[:, 0:128], on=ones8[0:1, 0:128],
                        r=scr["rows"][0:1, 512 + hp * 128:640 + hp * 128]:
                        t.matmul(o, on, r, start=True, stop=True),
                        reads=(("crow", hp), ("ones8",)), writes=(pk4,),
                        tag=f"cbc2{ci},{h}")
                    if L0A_HOPS < 20:
                        return
                    B.s(lambda e, dd=wt_["At"][:, hp, :], s=ps4[:, 0:128]:
                        e.activation(dd, s, AF.Exp),
                        reads=(pk4,), writes=(("At", hp),), tag=f"at{ci},{h}")
                    if L0A_HOPS < 205:
                        return
                    B.v(lambda e, dd=argx[:, hp, :], s=ps4[:, 0:128],
                        sc=ct24[:, h:h + 1]:
                        e.tensor_scalar(dd, s, sc, None, ALU.subtract),
                        reads=(pk4, ("ct24", ci)), writes=(("argx", hp),),
                        tag=f"ax{ci},{h}")
                    if L0A_HOPS < 206:
                        return
                    B.v(lambda e, dd=argx[:, hp, :], m=dn[:, 0, :]:
                        e.tensor_mul(dd, dd, m),
                        reads=(("argx", hp), ("dn",)), writes=(("argx", hp),),
                        tag=f"axm{ci},{h}")
                    if L0A_HOPS < 207:
                        return
                    B.v(lambda e, dd=wt_["eui"][:, hp, :], s=argx[:, hp, :],
                        m=dn[:, 1, :]: e.tensor_add(dd, s, m),
                        reads=(("argx", hp), ("dn",)), writes=(("eui", hp),),
                        tag=f"axb{ci},{h}")
                    if L0A_HOPS < 208:
                        return
                    B.s(lambda e, dd=wt_["eui"][:, hp, :]:
                        e.activation(dd, dd, AF.Exp),
                        reads=(("eui", hp),), writes=(("eui", hp),),
                        tag=f"euix{ci},{h}")
                    B.v(lambda e, dd=wt_["eus"][:, hp, :],
                        s=wt_["eui"][:, hp, :], m=dn[:, 2, :]:
                        e.tensor_mul(dd, s, m),
                        reads=(("eui", hp), ("dn",)), writes=(("eus", hp),),
                        tag=f"eus{ci},{h}")
                    if L0A_HOPS < 21:
                        return
                    B.v(lambda e, dd=argx[:, hp, :], s=ps4[:, 0:128],
                        sc=ct24[:, h:h + 1]:
                        e.tensor_scalar(dd, s, sc, None, ALU.subtract),
                        reads=(pk4, ("ct24", ci)), writes=(("argx", hp),),
                        tag=f"ax2{ci},{h}")
                    B.v(lambda e, dd=argx[:, hp, :], m=dn[:, 3, :]:
                        e.tensor_mul(dd, dd, m),
                        reads=(("argx", hp), ("dn",)), writes=(("argx", hp),),
                        tag=f"axm2{ci},{h}")
                    B.v(lambda e, dd=argx[:, hp, :], m=dn[:, 4, :]:
                        e.tensor_add(dd, dd, m),
                        reads=(("argx", hp), ("dn",)), writes=(("argx", hp),),
                        tag=f"axb2{ci},{h}")
                    B.s(lambda e, dd=wt_["b0"][:, hp, :], s=argx[:, hp, :]:
                        e.activation(dd, s, AF.Exp, scale=-1.0),
                        reads=(("argx", hp),), writes=(("b0", hp),),
                        tag=f"elt{ci},{h}")
                    if L0A_HOPS < 3:
                        return
                    ps5, pk5 = E.ps()
                    B.t(lambda t, o=ps5[0:1, 0:128], l=idn[0:8, h:h + 1],
                        r=cbb[:, 1, :]: t.matmul(o, l, r, start=True, stop=True),
                        reads=(("cbb", ci),), writes=(pk5,), tag=f"br{ci},{h}")
                    B.v(lambda e, dd=scr["rows_bf"][0:1, 128 + hp * 384:
                                                     256 + hp * 384],
                        s=ps5[0:1, 0:128]: e.tensor_copy(dd, s),
                        reads=(pk5,), writes=(("brow", hp),),
                        tag=f"bre{ci},{h}")
                    ps6, pk6 = E.ps()
                    B.t(lambda t, o=ps6[:, 0:128], on=ones[0:1, :],
                        r=scr["rows_bf"][0:1, 128 + hp * 384:256 + hp * 384]:
                        t.matmul(o, on, r, start=True, stop=True),
                        reads=(("brow", hp),), writes=(pk6,),
                        tag=f"bbc{ci},{h}")
                    B.v(lambda e, dd=wt_["a0"][:, hp, :],
                        a=wt_["eus"][:, hp, :], b=wt_["kk"][:, hp, :]:
                        e.tensor_mul(dd, a, b),
                        reads=(("eus", hp), ("kk", hp)), writes=(("a0", hp),),
                        tag=f"a01{ci},{h}")
                    B.v(lambda e, dd=wt_["a0"][:, hp, :],
                        s=wt_["a0"][:, hp, :], bb=ps6[:, 0:128]:
                        e.scalar_tensor_tensor(dd, s, -1.0, bb, ALU.mult,
                                               ALU.mult),
                        reads=(("a0", hp), pk6), writes=(("a0", hp),),
                        tag=f"a02{ci},{h}")
                    B.v(lambda e, dd=wt_["b0"][:, hp, :],
                        s=wt_["b0"][:, hp, :], b=wt_["kk"][:, hp, :]:
                        e.tensor_mul(dd, s, b),
                        reads=(("b0", hp), ("kk", hp)), writes=(("b0", hp),),
                        tag=f"b01{ci},{h}")
                    B.v(lambda e, dd=wt_["b0"][:, hp, :],
                        sc=nbcol[:, hp:hp + 1]:
                        e.tensor_scalar(dd, dd, sc, None, ALU.mult),
                        reads=(("b0", hp), ("nb", hp)), writes=(("b0", hp),),
                        tag=f"b02{ci},{h}")
                    if L0A_HOPS < 4:
                        return
                    psW, pkW = E.ps()
                    B.t(lambda t, o=psW[:, 0:128], l=kc_, r=S0b:
                        t.matmul(o, l, r, start=True, stop=True),
                        reads=rk_ + (kS0b,), writes=(pkW,), tag=f"W{ci},{h}")
                    psV, pkV = E.pst()
                    B.t(lambda t, o=psV[:, 0:128], s=vc_, i2=idn:
                        t.transpose(o, s, i2),
                        reads=rv, writes=(pkV,), tag=f"vt{ci},{h}")
                    B.v(lambda e, dd=wt_["vtm"][:, hp, :], s=psV[:, 0:128]:
                        e.tensor_copy(dd, s),
                        reads=(pkV,), writes=(("vtm", hp),), tag=f"vte{ci},{h}")
                    B.v(lambda e, dd=wt_["ub"][:, hp, :], s=psW[:, 0:128],
                        sc=ac24[:, h:h + 1], b=wt_["vtm"][:, hp, :]:
                        e.scalar_tensor_tensor(dd, s, sc, b, ALU.mult,
                                               ALU.subtract),
                        reads=(pkW, ("ac24", ci), ("vtm", hp)),
                        writes=(("ub", hp),), tag=f"R1{ci},{h}")
                    B.v(lambda e, dd=wt_["ub"][:, hp, :],
                        sc=nbcol[:, hp:hp + 1]:
                        e.tensor_scalar(dd, dd, sc, None, ALU.mult),
                        reads=(("ub", hp), ("nb", hp)), writes=(("ub", hp),),
                        tag=f"R2{ci},{h}")
                    if L0A_HOPS < 5:
                        return
                    B.v(lambda e, dd=tb[0][:, hp, :], a=wt_["a0"][:, hp, :],
                        b=idn[:, :]: e.tensor_add(dd, a, b),
                        reads=(("a0", hp),), writes=(("tb", 0, hp),),
                        tag=f"t0{ci},{h}")
                    B.v(lambda e, dd=ab[0][:, hp, :], s=wt_["a0"][:, hp, :]:
                        e.tensor_copy(dd, s),
                        reads=(("a0", hp),), writes=(("ab", 0, hp),),
                        tag=f"ta{ci},{h}")
                    B.v(lambda e, dd=bb_[0][:, hp, :], s=wt_["b0"][:, hp, :]:
                        e.tensor_copy(dd, s),
                        reads=(("b0", hp),), writes=(("bb", 0, hp),),
                        tag=f"tbb{ci},{h}")
                    for j in range(1, 7):
                        jp = j % 2
                        jq = 1 - jp
                        psA, pkA = E.ps()
                        B.t(lambda t, o=psA[:, 0:128], l=bb_[jq][:, hp, :],
                            r=ab[jq][:, hp, :]:
                            t.matmul(o, l, r, start=True, stop=True),
                            reads=(("bb", jq, hp), ("ab", jq, hp)),
                            writes=(pkA,), tag=f"ia{ci},{h},{j}")
                        B.v(lambda e, dd=ab[jp][:, hp, :], s=psA[:, 0:128]:
                            e.tensor_copy(dd, s),
                            reads=(pkA,), writes=(("ab", jp, hp),),
                            tag=f"iae{ci},{h},{j}")
                        psB, pkB = E.ps()
                        B.t(lambda t, o=psB[:, 0:128], l=ab[jq][:, hp, :],
                            r=bb_[jq][:, hp, :]:
                            t.matmul(o, l, r, start=True, stop=True),
                            reads=(("ab", jq, hp), ("bb", jq, hp)),
                            writes=(pkB,), tag=f"ib{ci},{h},{j}")
                        B.v(lambda e, dd=bb_[jp][:, hp, :], s=psB[:, 0:128]:
                            e.tensor_copy(dd, s),
                            reads=(pkB,), writes=(("bb", jp, hp),),
                            tag=f"ibe{ci},{h},{j}")
                        psT, pkT = E.ps()
                        B.t(lambda t, o=psT[:, 0:128], l=bb_[jp][:, hp, :],
                            r=tb[jq][:, hp, :]:
                            t.matmul(o, l, r, start=True, stop=True),
                            reads=(("bb", jp, hp), ("tb", jq, hp)),
                            writes=(pkT,), tag=f"it{ci},{h},{j}")
                        B.v(lambda e, dd=tb[jp][:, hp, :], s=psT[:, 0:128],
                            b=tb[jq][:, hp, :]: e.tensor_add(dd, s, b),
                            reads=(pkT, ("tb", jq, hp)),
                            writes=(("tb", jp, hp),), tag=f"ite{ci},{h},{j}")
                    if L0A_HOPS < 6:
                        return
                    psU, pkU = E.ps()
                    B.t(lambda t, o=psU[:, 0:128], l=tb[0][:, hp, :],
                        r=wt_["ub"][:, hp, :]:
                        t.matmul(o, l, r, start=True, stop=True),
                        reads=(("tb", 0, hp), ("ub", hp)), writes=(pkU,),
                        tag=f"U{ci},{h}")
                    B.v(lambda e, dd=wt_["ub"][:, hp, :], s=psU[:, 0:128]:
                        e.tensor_copy(dd, s),
                        reads=(pkU,), writes=(("ub", hp),),
                        tag=f"Ue{ci},{h}")
                    B.v(lambda e, dd=wt_["aq"][:, hp, :],
                        a=wt_["eui"][:, hp, :], b=wt_["qk"][:, hp, :]:
                        e.tensor_mul(dd, a, b),
                        reads=(("eui", hp), ("qk", hp)), writes=(("aq", hp),),
                        tag=f"aq{ci},{h}")
                    B.v(lambda e, dd=wt_["qA"][:, hp, :], a=qc_,
                        b=wt_["At"][:, hp, :]: e.tensor_mul(dd, a, b),
                        reads=rq + (("At", hp),), writes=(("qA", hp),),
                        tag=f"qA{ci},{h}")
                    psO, pkO = E.ps()
                    B.t(lambda t, o=psO[:, 0:128], l=wt_["ub"][:, hp, :],
                        r=wt_["aq"][:, hp, :]:
                        t.matmul(o, l, r, start=True, stop=False),
                        reads=(("ub", hp), ("aq", hp)), writes=(pkO,),
                        tag=f"O1{ci},{h}")
                    B.t(lambda t, o=psO[:, 0:128], l=S0b,
                        r=wt_["qA"][:, hp, :]:
                        t.matmul(o, l, r, start=False, stop=True),
                        reads=(kS0b, ("qA", hp)), writes=(pkO,),
                        tag=f"O2{ci},{h}")
                    if L0A_HOPS < 7:
                        return
                    B.s(lambda e, dd=wt_["rn"][:, hp, :], s=psO[:, 0:128]:
                        e.activation(dd, s, AF.Square),
                        reads=(pkO,), writes=(("o2", hp),), tag=f"o2{ci},{h}")
                    psq, pkq = E.ps()
                    B.t(lambda t, o=psq[0:1, 0:128], on=ones[:, 0:1],
                        x=wt_["rn"][:, hp, :]:
                        t.matmul(o, on, x, start=True, stop=True),
                        reads=(("o2", hp),), writes=(pkq,), tag=f"oq{ci},{h}")
                    orsl = slice(256 + hp * 128, 384 + hp * 128)
                    orbsl = slice(256 + hp * 384, 384 + hp * 384)
                    B.v(lambda e, dd=scr["rows"][0:1, orsl],
                        s=psq[0:1, 0:128]:
                        e.tensor_scalar(dd, s, 1.0 / DV, None, ALU.mult),
                        reads=(pkq,), writes=(("orn", hp),),
                        tag=f"or1{ci},{h}")
                    B.s(lambda e, dd=scr["rows"][0:1, orsl],
                        ba=E.scal(1e-6):
                        e.activation(dd, dd, AF.Ln, bias=ba, scale=1.0),
                        reads=(("orn", hp),), writes=(("orn", hp),),
                        tag=f"or2{ci},{h}")
                    B.s(lambda e, dd=scr["rows_bf"][0:1, orbsl],
                        s=scr["rows"][0:1, orsl]:
                        e.activation(dd, s, AF.Exp, scale=-0.5),
                        reads=(("orn", hp),), writes=(("ornb", hp),),
                        tag=f"or5{ci},{h}")
                    psr, pkr = E.ps()
                    B.t(lambda t, o=psr[:, 0:128], on=ones[0:1, :],
                        r=scr["rows_bf"][0:1, orbsl]:
                        t.matmul(o, on, r, start=True, stop=True),
                        reads=(("ornb", hp),), writes=(pkr,),
                        tag=f"orb{ci},{h}")
                    B.v(lambda e, dd=wt_["rn"][:, hp, :], s=psr[:, 0:128]:
                        e.tensor_copy(dd, s),
                        reads=(pkr,), writes=(("rn", hp),), tag=f"orbe{ci},{h}")
                    B.dma("sync", ztile[:, hp, :],
                          z_dram[h * 128:(h + 1) * 128, csl],
                          writes=(("zt", hp),), tag=f"zl{ci},{h}")
                    B.s(lambda e, dd=wt_["zs"][:, hp, :], s=ztile[:, hp, :]:
                        e.activation(dd, s, AF.Silu),
                        reads=(("zt", hp),), writes=(("zs", hp),),
                        tag=f"zs{ci},{h}")
                    B.v(lambda e, dd=wt_["co"][:, hp, :], s=psO[:, 0:128],
                        sc=cols[:, 16:17], b=wt_["rn"][:, hp, :]:
                        e.scalar_tensor_tensor(dd, s, sc, b, ALU.mult,
                                               ALU.mult),
                        reads=(pkO, ("cols",), ("rn", hp)),
                        writes=(("co", hp),), tag=f"co1{ci},{h}")
                    B.v(lambda e, dd=wt_["co"][:, hp, :],
                        b=wt_["zs"][:, hp, :]: e.tensor_mul(dd, dd, b),
                        reads=(("co", hp), ("zs", hp)), writes=(("co", hp),),
                        tag=f"co2{ci},{h}")
                    B.dma("scalar", core_dram[h * 128:(h + 1) * 128, csl],
                          wt_["co"][:, hp, :],
                          reads=(("co", hp),), tag=f"cod{ci},{h}")
                    if L0A_HOPS < 8:
                        return
                    B.v(lambda e, dd=wt_["ktm"][:, hp, :], s=None: None,
                        reads=(), writes=(), tag="x") if False else None
                    psK, pkK = E.pst()
                    B.t(lambda t, o=psK[:, 0:128], s=kc_, i2=idn:
                        t.transpose(o, s, i2),
                        reads=rk_, writes=(pkK,), tag=f"kt{ci},{h}")
                    B.v(lambda e, dd=wt_["ktm"][:, hp, :], s=psK[:, 0:128],
                        sc=ac24[:, 16 + h:17 + h]:
                        e.tensor_scalar(dd, s, sc, None, ALU.mult),
                        reads=(pkK, ("ac24", ci)), writes=(("ktm", hp),),
                        tag=f"kte{ci},{h}")
                    psS, pkS = E.ps()
                    B.t(lambda t, o=psS[:, 0:128], l=wt_["ktm"][:, hp, :],
                        r=wt_["ub"][:, hp, :]:
                        t.matmul(o, l, r, start=True, stop=True),
                        reads=(("ktm", hp), ("ub", hp)), writes=(pkS,),
                        tag=f"S{ci},{h}")
                    B.v(lambda e, dd=Sf[:, par, h * DV:(h + 1) * DV], s=S0f,
                        sc=acbc[:, h:h + 1], b=psS[:, 0:128]:
                        e.scalar_tensor_tensor(dd, s, sc, b, ALU.mult, ALU.add),
                        reads=(("Sf", (ci + 1) % 2), ("acbc", ci), pkS),
                        writes=(("Sf", par),), tag=f"Sn{ci},{h}")
                    B.v(lambda e, dd=Sb[:, par, h * DV:(h + 1) * DV],
                        s=Sf[:, par, h * DV:(h + 1) * DV]: e.tensor_copy(dd, s),
                        reads=(("Sf", par),), writes=(("Sb", par),),
                        tag=f"Sc{ci},{h}")
                    if ci == 0 and h == 0:
                        for nm in ("kk", "qk", "eui", "eus", "a0", "b0",
                                   "ub", "aq", "qA", "At", "vtm", "ktm", "co"):
                            E.dump("sc_" + nm, wt_[nm][:, 0, :])
                        E.dump("sc_tb", tb[0][:, 0, :])
                        E.dump("sc_S", Sf[:, 0, 0:128])

                def lane_body(hh, banks, pbank):
                    def run():
                        E.psf_set = banks
                        E.psb_set = pbank
                        head_body(hh)
                        E.psf_set = None
                        E.psb_set = None
                    return run

                nhh = min(HV, L0A_NHEADS)
                for hg in range(0, nhh, 2):
                    if hg + 1 < nhh and not L0A_SERIAL:
                        B.braid([lane_body(hg, [0, 1, 2], [0]),
                                 lane_body(hg + 1, [3, 4, 5], [1])])
                    else:
                        head_body(hg)
                        if hg + 1 < nhh:
                            head_body(hg + 1)
                    if L0A_SERIAL:
                        B.barrier()
            B.barrier()
        s_C.close()
        s_ba.close()

        if STAGE < 4:
            xs = E.sb("l0a_xs2", [128, KC, T], F32, s_B, side="right")
            load_fm_f32(B, E, xs, "xs", spill, tag="l0aus")
            B.barrier()
            with ExitStack() as c_f:
                fin2 = E.sb("l0a_fin", [128, 2, 512], F32, c_f)
                emit_residual_finale(B, E, xs, xsk, src_dram, dst_dram, fin2,
                                     tag="l0a_fin")
                B.barrier()
            s_B.close()
            return
        s_E = ExitStack()
        coreb = E.sb("l0a_coreb", [128, KC, T], BF16, s_E)
        xs = E.sb("l0a_xs2", [128, KC, T], F32, s_B, side="right")
        for kc in range(KC):
            B.dma("sync", coreb[:, kc, :], core_dram[kc * 128:(kc + 1) * 128, :],
                  writes=(("coreb", kc, 0), ("coreb", kc, 1),
                          ("coreb", kc, 2), ("coreb", kc, 3)),
                  tag=f"crl{kc}")
        load_fm_f32(B, E, xs, "xs", spill, tag="l0aus")
        E.dump("ocore", coreb[:, :, :].rearrange("p a b -> p (a b)"))

        def ep_wo(oc, tt, ps, pk, ow):
            B.v(lambda e, dd=xs[:, oc, tt * 512:(tt + 1) * 512], s=ps:
                e.tensor_add(dd, dd, s),
                reads=(pk, ("xs", oc, tt)), writes=(("xs", oc, tt),),
                tag="epwo")
        stream_project_fm(B, E, None, d["L0_attn_W_out"], 1024, 1024,
                          coreb, lambda kc, tt: ("coreb", kc, tt), ep_wo,
                          tag="l0awo")
        B.barrier()
        E.dump("oh", xs[:, :, :].rearrange("p a b -> p (a b)"))
        s_E.close()
        if STAGE < 5:
            with ExitStack() as c_f:
                fin2 = E.sb("l0a_fin", [128, 2, 512], F32, c_f)
                emit_residual_finale(B, E, xs, xsk, src_dram, dst_dram, fin2,
                                     tag="l0a_fin")
                B.barrier()
            s_B.close()
            return

        with ExitStack() as c_moe:
            xn2 = E.sb("l0a_xn2", [128, KC, T], BF16, c_moe)
            rt_fm = E.sb("l0a_rtfm", [8, T], BF16, c_moe)
            mkbc = E.sb("l0a_mkbc", [128, 4, 512], BF16, c_moe)
            sm8 = E.sb("l0a_sm8", [128, 6], F32, c_moe)
            emit_ln(B, E, xs, xsk, xn2, "xn2", KC, T, kind="rms",
                    g_col=cols[:, 8:16], g_key=("cols",), eps=1e-6, scr=scr,
                    src_is_f32=True, tag="l0a_rms2")
            xnk2 = lambda kc, tt: ("xn2", kc, tt)
            with ExitStack() as c_r:
                rw = E.sb("l0a_rw", [128, KC, 8], BF16, c_r)
                lgt = E.sb("l0a_lgt", [8, T], F32, c_r)
                prb = E.sb("l0a_prb", [128, 8], F32, c_r)
                prb2 = E.sb("l0a_prb2", [128, 8], F32, c_r)
                rt_tm = E.sb("l0a_rttm", [128, 8], BF16, c_r)
                B.dma("gpsimd", rw[:, :, :],
                      d["L0_attn_W_router"].rearrange("(k p) n -> p k n", p=128),
                      writes=(("rw",),))
                for tt in range(4):
                    psl, pkl = E.ps()
                    parts = [(rw[:, kc2, 0:8],
                              xn2[:, kc2, tt * 512:(tt + 1) * 512])
                             for kc2 in range(KC)]
                    rk = [("rw",)] + [("xn2", kc2, tt) for kc2 in range(KC)]
                    mm_group(B, psl[0:8, 0:512], pkl, parts, rk, tag=f"rt{tt}")
                    B.v(lambda e, dd=lgt[0:8, tt * 512:(tt + 1) * 512],
                        s=psl[0:8, 0:512]: e.tensor_copy(dd, s),
                        reads=(pkl,), writes=(("lgt", tt),), tag=f"rte{tt}")
                E.dump("olg", lgt[0:8, :])
                for tb2 in range(16):
                    tsl = slice(tb2 * 128, (tb2 + 1) * 128)
                    pst, pkt = E.ps()
                    B.t(lambda t, o=pst[0:128, 0:8], s=lgt[0:8, tsl],
                        i2=idf[0:8, 0:8]: t.transpose(o, s, i2),
                        reads=(("lgt", tb2 // 4),), writes=(pkt,), tag=f"pt{tb2}")
                    B.v(lambda e, dd=sm8[:, 0:1], s=pst[0:128, 0:8]:
                        e.reduce_max(dd, s, axis=AX.X, negate=True),
                        reads=(pkt,), writes=(("pm",),), tag=f"pm{tb2}")
                    B.s(lambda e, dd=prb[:, 0:8], s=pst[0:128, 0:8],
                        bb=sm8[:, 0:1], a=sm8[:, 1:2]:
                        e.activation(dd, s, AF.Exp, bias=bb, scale=1.0,
                                     accum_out=a),
                        reads=(pkt, ("pm",)), writes=(("prb",), ("psm",)),
                        tag=f"pe{tb2}")
                    B.v(lambda e, dd=sm8[:, 1:2]: e.reciprocal(dd, dd),
                        reads=(("psm",),), writes=(("psm",),), tag=f"pr{tb2}")
                    B.v(lambda e, dd=prb[:, 0:8], sc=sm8[:, 1:2]:
                        e.tensor_scalar(dd, dd, sc, None, ALU.mult),
                        reads=(("prb",), ("psm",)), writes=(("prb",),),
                        tag=f"pn{tb2}")
                    B.v(lambda e, dd=sm8[:, 2:3], s=prb[:, 0:8]:
                        e.reduce_max(dd, s, axis=AX.X),
                        reads=(("prb",),), writes=(("m1",),), tag=f"m1{tb2}")
                    B.v(lambda e, dd=prb2[:, 0:8], s=prb[:, 0:8],
                        sc=sm8[:, 2:3]:
                        e.tensor_scalar(dd, s, sc, None, ALU.is_ge),
                        reads=(("prb",), ("m1",)), writes=(("eq1",),),
                        tag=f"eq{tb2}")
                    B.v(lambda e, dd=sm8[:, 3:4], s=prb[:, 0:8],
                        m=rt_tm[:, 0:8]:
                        e.scalar_tensor_tensor(dd, m, -1.0, s, ALU.mult,
                                               ALU.bypass) if False else None,
                        reads=(), writes=(), tag="x") if False else None
                    B.v(lambda e, dd=sm8[:, 4:5], s=rt_tm[:, 0:8]:
                        e.tensor_copy(dd, s) if False else None,
                        reads=(), writes=(), tag="x") if False else None
                    B.v(lambda e, dd=prb2[:, 0:8]:
                        e.tensor_scalar(dd, dd, -1.0, 1.0, ALU.mult, ALU.add),
                        reads=(("eq1",),), writes=(("eq1",),), tag=f"inv{tb2}")
                    B.v(lambda e, dd=prb2[:, 0:8], s=prb[:, 0:8]:
                        e.tensor_mul(dd, dd, s),
                        reads=(("eq1",), ("prb",)), writes=(("eq1",),),
                        tag=f"pm2{tb2}")
                    B.v(lambda e, dd=sm8[:, 3:4], s=prb2[:, 0:8]:
                        e.reduce_max(dd, s, axis=AX.X),
                        reads=(("eq1",),), writes=(("m2",),), tag=f"m2{tb2}")
                    B.v(lambda e, dd=sm8[:, 4:5], a=sm8[:, 2:3], b=sm8[:, 3:4]:
                        e.tensor_add(dd, a, b),
                        reads=(("m1",), ("m2",)), writes=(("ws",),),
                        tag=f"ws{tb2}")
                    B.v(lambda e, dd=sm8[:, 4:5]: e.reciprocal(dd, dd),
                        reads=(("ws",),), writes=(("ws",),), tag=f"wr{tb2}")
                    B.v(lambda e, dd=rt_tm[:, 0:8], s=prb[:, 0:8],
                        sc=sm8[:, 3:4]:
                        e.tensor_scalar(dd, s, sc, None, ALU.is_ge),
                        reads=(("prb",), ("m2",)), writes=(("rt",),),
                        tag=f"ge2{tb2}")
                    B.v(lambda e, dd=rt_tm[:, 0:8], s=prb[:, 0:8]:
                        e.tensor_mul(dd, dd, s),
                        reads=(("rt",), ("prb",)), writes=(("rt",),),
                        tag=f"rp{tb2}")
                    B.v(lambda e, dd=rt_tm[:, 0:8], sc=sm8[:, 4:5]:
                        e.tensor_scalar(dd, dd, sc, None, ALU.mult),
                        reads=(("rt",), ("ws",)), writes=(("rt",),),
                        tag=f"rn{tb2}")
                    psm2, pkm2 = E.pst()
                    B.t(lambda t, o=psm2[0:8, 0:128], s=rt_tm[:, 0:8], i2=idn:
                        t.transpose(o, s, i2),
                        reads=(("rt",),), writes=(pkm2,), tag=f"rtt{tb2}")
                    B.v(lambda e, dd=rt_fm[0:8, tsl], s=psm2[0:8, 0:128]:
                        e.tensor_copy(dd, s),
                        reads=(pkm2,), writes=(("rtfm", tb2),), tag=f"rtf{tb2}")
            B.barrier()
            E.dump("ort", rt_fm[0:8, :])
            gq = E.sb("l0a_gq", [128, 4, T], BF16, c_moe)
            uq = E.sb("l0a_uq", [128, 4, T], BF16, c_moe)

            wg = d["L0_attn_W_gate_e"]
            wu = d["L0_attn_W_up_e"]
            wd = d["L0_attn_W_down_e"]
            for e_i in range(8):
                for tt in range(4):
                    tsl = slice(tt * 512, (tt + 1) * 512)
                    ps1, pk1 = E.ps()
                    B.t(lambda t, o=ps1[0:1, 0:512], l=idn[0:8, e_i:e_i + 1],
                        r=rt_fm[0:8, tsl]:
                        t.matmul(o, l, r, start=True, stop=True),
                        reads=tuple(("rtfm", tb2) for tb2 in
                                    range(tt * 4, tt * 4 + 4)),
                        writes=(pk1,), tag=f"mr{e_i},{tt}")
                    B.v(lambda e, dd=scr["rows_bf"][0:1, 0:512],
                        s=ps1[0:1, 0:512]: e.tensor_copy(dd, s),
                        reads=(pk1,), writes=(("mrow",),), tag=f"mre{e_i},{tt}")
                    ps2, pk2 = E.ps()
                    B.t(lambda t, o=ps2[:, 0:512], on=ones[0:1, :],
                        r=scr["rows_bf"][0:1, 0:512]:
                        t.matmul(o, on, r, start=True, stop=True),
                        reads=(("mrow",),), writes=(pk2,), tag=f"mb{e_i},{tt}")
                    B.op(E.rr(), (lambda e, dd=mkbc[:, tt, :], s=ps2[:, 0:512]:
                                  ecopy(e, dd, s)),
                         reads=(pk2,), writes=(("mkbc", tt),),
                         tag=f"mbe{e_i},{tt}")

                def ep_g(oc, tt, ps, pk, ow):
                    B.s(lambda e, dd=gq[:, oc, tt * 512:(tt + 1) * 512], s=ps:
                        e.activation(dd, s, AF.Silu),
                        reads=(pk,), writes=(("gq", oc, tt),), tag="epg")

                def ep_u(oc, tt, ps, pk, ow):
                    B.v(lambda e, dd=uq[:, oc, tt * 512:(tt + 1) * 512], s=ps,
                        m=mkbc[:, tt, :]: e.tensor_mul(dd, s, m),
                        reads=(pk, ("mkbc", tt)), writes=(("uq", oc, tt),),
                        tag="epu")
                stream_project_fm(B, E, None, wg[e_i], 1024, 512,
                                  xn2, xnk2, ep_g, tag=f"ge{e_i}")
                stream_project_fm(B, E, None, wu[e_i], 1024, 512,
                                  xn2, xnk2, ep_u, tag=f"ue{e_i}")
                for oc in range(4):
                    for tt in range(4):
                        tsl = slice(tt * 512, (tt + 1) * 512)
                        B.v(lambda e, dd=gq[:, oc, tsl], v2=uq[:, oc, tsl]:
                            e.tensor_mul(dd, dd, v2),
                            reads=(("gq", oc, tt), ("uq", oc, tt)),
                            writes=(("gq", oc, tt),), tag=f"gu{e_i}{oc}{tt}")
                wtile = E.wtile(4, 1024)
                B.dma("gpsimd", wtile[0][:, :, :],
                      wd[e_i].rearrange("(k p) n -> p k n", p=128),
                      writes=(wtile[1],), tag=f"wd{e_i}")
                for oc in range(8):
                    for tt in range(4):
                        psj, pkj = E.ps()
                        parts = [(wtile[0][:, kc2, oc * 128:(oc + 1) * 128],
                                  gq[:, kc2, tt * 512:(tt + 1) * 512])
                                 for kc2 in range(4)]
                        rk = [wtile[1]] + [("gq", kc2, tt) for kc2 in range(4)]
                        mm_group(B, psj[:, 0:512], pkj, parts, rk,
                                 tag=f"dn{e_i}{oc}{tt}")
                        B.v(lambda e, dd=xs[:, oc, tt * 512:(tt + 1) * 512],
                            s=psj[:, 0:512]: e.tensor_add(dd, dd, s),
                            reads=(pkj, ("xs", oc, tt)),
                            writes=(("xs", oc, tt),), tag="dna")
            B.barrier()
        with ExitStack() as c_f:
            fin2 = E.sb("l0a_fin", [128, 2, 512], F32, c_f)
            emit_residual_finale(B, E, xs, xsk, src_dram, dst_dram, fin2,
                                 tag="l0a_fin")
            B.barrier()
        s_B.close()



from concourse.bass_utils import run_bass_kernel_spmd


def kernel(**inputs):
    nc, consts = setup_program(
        [phase_l0a, phase_l0f, phase_l1a, phase_l1f])
    in_maps = [host_inputs_for(inputs, consts, b) for b in range(8)]
    res = run_bass_kernel_spmd(nc, in_maps, core_ids=list(range(8)))
    y = np.stack([np.asarray(res.results[b]["y"]).T for b in range(8)])
    return np.ascontiguousarray(y.astype(np.float32))
